# version 06 (final)
# speedup vs baseline: 1.1585x; 1.0531x over previous
import contextlib
import numpy as np

import concourse.bass as bass
import concourse.mybir as mybir
from concourse.bass_utils import run_bass_kernel_spmd

F32 = mybir.dt.float32
BF16 = mybir.dt.bfloat16
I32 = mybir.dt.int32
U8 = mybir.dt.uint8
AF = mybir.ActivationFunctionType
ALU = mybir.AluOpType
AX = mybir.AxisListType

D = 1024
TOK = 2048
NDT = D // 128
NTT = TOK // 128

COMPUTE_ENGINES = ("pe", "act", "dve", "pool")
DMA_QUEUES = ("sp", "actq", "poolq", "dveq")
ALL_ENGINES = COMPUTE_ENGINES + DMA_QUEUES

_ENGINE_SECTION = {
    "pe": "tensor", "act": "scalar", "dve": "vector", "pool": "gpsimd",
    "sp": "sync", "actq": "scalar", "poolq": "gpsimd", "dveq": "vector",
}


class Step:
    def __init__(self, eng, kind, ordinal, emit, deps, tag, gseq,
                 sync_after=False):
        self.eng = eng
        self.kind = kind
        self.ord = ordinal
        self.emit = emit
        self.deps = deps
        self.tag = tag
        self._gseq = gseq
        self.sync_after = sync_after


class Prog:
    def __init__(self, nc):
        self.nc = nc
        self.seq = {e: {"c": [], "d": []} for e in ALL_ENGINES}
        self.order = {e: [] for e in ALL_ENGINES}
        self.last_write = {}
        self.readers = {}
        self.gseq = 0
        self.fence_deps = []

    def _mkstep(self, eng, kind, emit, reads, writes, tag, sync_after=False):
        deps = list(self.fence_deps)
        for b in reads:
            w = self.last_write.get(b)
            if w is not None:
                deps.append(w)
        for b in writes:
            w = self.last_write.get(b)
            if w is not None:
                deps.append(w)
            deps.extend(self.readers.get(b, ()))
        st = Step(eng, kind, len(self.seq[eng][kind]), emit, deps, tag,
                  self.gseq, sync_after=sync_after)
        self.gseq += 1
        self.seq[eng][kind].append(st)
        self.order[eng].append(st)
        for b in reads:
            self.readers.setdefault(b, []).append(st)
        for b in writes:
            self.last_write[b] = st
            self.readers[b] = [st]
        return st

    def step(self, eng, emit, reads=(), writes=(), tag="", sync_after=False):
        assert eng in COMPUTE_ENGINES
        return self._mkstep(eng, "c", emit, reads, writes, tag,
                            sync_after=sync_after)

    def dma(self, out_ap, in_ap, reads=(), writes=(), queue="sp", tag=""):
        def emit(h):
            return h.dma_start(out=out_ap, in_=in_ap)
        return self._mkstep(queue, "d", emit, reads, writes, tag)

    def fence(self):
        tails = []
        for e in ALL_ENGINES:
            if self.order[e]:
                tails.append(self.order[e][-1])
        self.fence_deps = tails

    def finalize(self, stack):
        nc = self.nc
        used = [(e, k) for e in ALL_ENGINES for k in ("c", "d")
                if self.seq[e][k]]
        sems = {}
        for key in used:
            n = len(self.seq[key[0]][key[1]]) * (16 if key[1] == "d" else 1)
            assert n < 500_000_000, f"semaphore overflow {key}: {n}"
            sems[key] = stack.enter_context(nc.semaphore(f"s_{key[0]}{key[1]}"))
        blk = stack.enter_context(nc.Block())
        final_waits = [(key, len(self.seq[key[0]][key[1]]) *
                        (16 if key[1] == "d" else 1)) for key in used]

        phys = {}
        for eng in ALL_ENGINES:
            if self.order[eng]:
                phys.setdefault(_ENGINE_SECTION[eng], []).extend(
                    [(eng, st) for st in self.order[eng]])

        for section, items in phys.items():
            items.sort(key=lambda p: p[1]._gseq)

            def run(h, items=items, section=section):
                waited = {}
                for eng, st in items:
                    for dep in st.deps:
                        same_stream = _ENGINE_SECTION[dep.eng] == section
                        if same_stream and dep.kind == "c":
                            continue
                        val = (dep.ord + 1) * (16 if dep.kind == "d" else 1)
                        key = (dep.eng, dep.kind)
                        if waited.get(key, -1) >= val:
                            continue
                        h.wait_ge(sems[key], val)
                        waited[key] = val
                    last = st.emit(h)
                    assert last is not None, f"step {st.tag} emitted nothing"
                    last.then_inc(sems[(st.eng, st.kind)],
                                  16 if st.kind == "d" else 1)
                    if st.sync_after:
                        h.wait_ge(sems[(st.eng, st.kind)],
                                  (st.ord + 1) * (16 if st.kind == "d" else 1))
                        waited[(st.eng, st.kind)] = (st.ord + 1) * (
                            16 if st.kind == "d" else 1)
                if section == "gpsimd":
                    for key, val in final_waits:
                        h.wait_ge(sems[key], val)
            getattr(blk, section)(run)
        if "gpsimd" not in phys:
            def run_final(h):
                for key, val in final_waits:
                    h.wait_ge(sems[key], val)
            blk.gpsimd(run_final)


class Ctx:
    def __init__(self, nc, prog, stack):
        self.nc = nc
        self.prog = prog
        self.stack = stack
        self.sb = {}
        self.dram = {}
        self.sb_w = {}

    def sbuf(self, name, shape, dtype):
        h = self.stack.enter_context(self.nc.sbuf_tensor(name, shape, dtype))
        self.sb[name] = h
        return h

    def psum(self, name, shape, dtype=F32):
        h = self.stack.enter_context(self.nc.psum_tensor(name, shape, dtype))
        self.sb[name] = h
        return h

    def psb(self, bank, width=1, part=128):
        return self.sb["PS"][0:part, bank * 512:(bank + width) * 512]


CONST_COLS = {
    "one": 1.0, "eps_ln": 1e-5, "eps_rms": 1e-6, "half": 0.5,
    "invD": 1.0 / 1024.0, "pi": float(np.pi), "neg30": -1e30,
    "eps_rms8": 8e-6,
    "eps_sq": 1e-10,
    "four": 4.0,
    "two": 2.0,
    "ln2": float(np.log(2.0)),
    "pihalf": float(np.pi / 2),
}
CONST_IDX = {k: i for i, k in enumerate(CONST_COLS)}


def make_const_host():
    v = np.array(list(CONST_COLS.values()), dtype=np.float32)
    return np.tile(v[None, :], (128, 1))


def make_ident_host():
    return np.eye(128, dtype=np.float32)


def ccol(ctx, name):
    return ctx.sb["consts"][:, CONST_IDX[name]:CONST_IDX[name] + 1]


def _bufnames(prefix, n):
    return [f"{prefix}{i}" for i in range(n)]


def alloc_common(ctx, bf_groups=2):
    for i in range(NDT):
        ctx.sbuf(f"xT{i}", [128, TOK], F32)
    names = ["bfA", "bfB"][:bf_groups] if bf_groups > 0 else []
    for nm in names:
        for i in range(NDT):
            ctx.sbuf(f"{nm}{i}", [128, TOK], BF16)
    ctx.sbuf("scrA", [128, TOK], F32)
    ctx.sb["wstg"] = ctx.sb["scrA"]
    ctx.sbuf("consts", [128, len(CONST_COLS)], F32)
    ctx.sb["rowA"] = ctx.sb["scrA"][0:1, 512:1024]
    ctx.sb["rowB"] = ctx.sb["scrA"][0:1, 1024:1536]
    ctx.sbuf("r_mub", [128, 512], F32)
    ctx.sbuf("r_rstdb", [128, 512], F32)
    ctx.sbuf("ones_f32", [128, 1], F32)
    ctx.psum("PS", [128, 4096], F32)
    ctx.dram["bc_scr"] = ctx.nc.dram_tensor(
        "bc_scr", [4, TOK], F32, kind="Internal").ap()
    ctx.prog.step("dve", lambda h: ctx.nc.vector.memset(
        ctx.sb["ones_f32"][:], 1.0), writes=["ones_f32"])


def load_stream_inputs(ctx, xin):
    for i in range(NDT):
        ctx.prog.dma(ctx.sb[f"xT{i}"][:], xin[i * 128:(i + 1) * 128, :],
                     writes=[f"xT{i}"])


def store_stream_outputs(ctx, xout):
    for i in range(NDT):
        ctx.prog.dma(xout[i * 128:(i + 1) * 128, :], ctx.sb[f"xT{i}"][:],
                     reads=[f"xT{i}"])



def emit_layernorm(ctx, x_bufs, xT, out_bf, out_bufs, tag):
    nc, prog = ctx.nc, ctx.prog
    ones = ctx.sb["ones_f32"]
    scr = ctx.sb["scrA"]
    rowA, rowB = ctx.sb["rowA"], ctx.sb["rowB"]
    bcd = ctx.dram["bc_scr"]
    p1 = ctx.psb(6, 1, part=1)
    p2 = ctx.psb(7, 1, part=1)

    for j in range(4):
        for i in range(NDT):
            prog.step("act",
                      lambda h, i=i, j=j: nc.scalar.activation(
                          out=scr[:, 0:512],
                          in_=xT[i][:, j * 512:(j + 1) * 512], func=AF.Square),
                      reads=[x_bufs[i]], writes=["scrA"], tag=tag)

            def mm_s(h, i=i, j=j):
                nc.tensor.matmul(
                    p1, ones[:, 0:1], xT[i][:, j * 512:(j + 1) * 512],
                    start=(i == 0), stop=(i == NDT - 1))
                return nc.tensor.matmul(
                    p2, ones[:, 0:1], scr[:, 0:512],
                    start=(i == 0), stop=(i == NDT - 1))
            prog.step("pe", mm_s, reads=list(x_bufs) + ["ones_f32", "scrA"],
                      writes=["ps6", "ps7", "scrA"], tag=tag + "s")

        def row_ops(h, j=j):
            nc.vector.tensor_scalar_mul(rowA, p1, 1.0 / D)
            nc.vector.tensor_scalar_mul(rowB, p2, 1.0 / D)
            nc.vector.tensor_mul(p2[0:1, 0:512], rowA, rowA)
            return nc.vector.tensor_sub(rowB, rowB, p2[0:1, 0:512])
        prog.step("dve", row_ops, reads=["ps6", "ps7"],
                  writes=["scrA", "ps6", "ps7"], tag=tag)
        prog.step("act",
                  lambda h: nc.scalar.activation(
                      out=rowB, in_=rowB, func=AF.Sqrt,
                      bias=ccol(ctx, "eps_ln")[0:1, :]),
                  reads=["scrA", "consts"], writes=["scrA"], tag=tag)
        prog.step("dve",
                  lambda h: nc.vector.reciprocal(rowB, rowB),
                  reads=["scrA"], writes=["scrA"], tag=tag)
        prog.dma(bcd[0:1, j * 512:(j + 1) * 512], rowA,
                 reads=["scrA"], writes=["bc0"], tag=tag)
        prog.dma(bcd[1:2, j * 512:(j + 1) * 512], rowB,
                 reads=["scrA"], writes=["bc1"], tag=tag)

    r_mub, r_rstdb = ctx.sb["r_mub"], ctx.sb["r_rstdb"]
    for j in range(4):
        prog.dma(r_mub[:, 0:512],
                 bcd[0:1, j * 512:(j + 1) * 512].broadcast_to([128, 512]),
                 reads=["bc0"], writes=["r_mub"], tag=tag)
        prog.dma(r_rstdb[:, 0:512],
                 bcd[1:2, j * 512:(j + 1) * 512].broadcast_to([128, 512]),
                 reads=["bc1"], writes=["r_rstdb"], tag=tag)
        for i in range(NDT):
            def apply(h, i=i, j=j):
                nc.vector.tensor_sub(scr[:, 0:512],
                                     xT[i][:, j * 512:(j + 1) * 512],
                                     r_mub[:, 0:512])
                return nc.vector.tensor_tensor(
                    out=out_bf[i][:, j * 512:(j + 1) * 512],
                    in0=scr[:, 0:512], in1=r_rstdb[:, 0:512], op=ALU.mult)
            prog.step("dve", apply,
                      reads=[x_bufs[i], "r_mub", "r_rstdb"],
                      writes=[out_bufs[i], "scrA"], tag=tag)



def load_weight_bf16(ctx, name, dram_ap, k_tiles, out_dim, queue="sp",
                     reuse=None):
    nc, prog = ctx.nc, ctx.prog
    stg = ctx.sb["wstg"]
    tiles = []
    for k in range(k_tiles):
        t = reuse[0][k] if reuse is not None else \
            ctx.sbuf(f"{name}_b{k}", [128, out_dim], BF16)
        tiles.append(t)
        prog.dma(stg[:, 0:out_dim], dram_ap[k * 128:(k + 1) * 128, :],
                 writes=["scrA"], queue=queue, tag=name)
        wnames = [name, "scrA"] if reuse is None else [name, reuse[1], "scrA"]
        prog.step("dve",
                  lambda h, k=k, t=t: nc.vector.tensor_copy(
                      out=t[:], in_=stg[:, 0:out_dim]),
                  reads=["scrA"], writes=wnames, tag=name)
    ctx.sb_w[name] = tiles
    return tiles


def precast_to_dram(ctx, name, src_f32_ap, dst_bf_ap, n_rows, n_cols,
                    cast_eng="pool"):
    nc, prog = ctx.nc, ctx.prog
    ee = {"pool": "gpsimd", "dve": "vector", "act": "scalar"}[cast_eng]
    if "castf0" in ctx.sb:
        cnt = ctx.__dict__.setdefault("_pc_cnt", [0])
        for k in range(n_rows // 128):
            for c0 in range(0, n_cols, 512):
                ce = min(n_cols, c0 + 512)
                p = cnt[0] % 4
                cnt[0] += 1
                eng, een = (("pool", "gpsimd") if p == 0
                            else ("dve", "vector"))
                stg = ctx.sb[f"castf{p}"]
                stb = ctx.sb[f"castb{p}"]
                prog.dma(stg[:, 0:ce - c0],
                         src_f32_ap[k * 128:(k + 1) * 128, c0:ce],
                         writes=[f"castf{p}"], queue="sp", tag=name)
                prog.step(eng,
                          lambda h, a=ce - c0, stb=stb, stg=stg, een=een:
                          getattr(nc, een).tensor_copy(
                              out=stb[:, 0:a], in_=stg[:, 0:a]),
                          reads=[f"castf{p}"], writes=[f"castb{p}"],
                          tag=name)
                prog.dma(dst_bf_ap[k * 128:(k + 1) * 128, c0:ce],
                         stb[:, 0:ce - c0],
                         reads=[f"castb{p}"], writes=[name], queue="actq",
                         tag=name)
        return
    stg = ctx.sb["castf"]
    stb = ctx.sb["castb"]
    cw = stb.shape[1] if hasattr(stb, "shape") else 512
    for k in range(n_rows // 128):
        for c0 in range(0, n_cols, cw):
            ce = min(n_cols, c0 + cw)
            prog.dma(stg[:, 0:ce - c0],
                     src_f32_ap[k * 128:(k + 1) * 128, c0:ce],
                     writes=["scrA"], queue="poolq", tag=name)
            prog.step(cast_eng,
                      lambda h, a=ce - c0: getattr(nc, ee).tensor_copy(
                          out=stb[:, 0:a], in_=stg[:, 0:a]),
                      reads=["scrA"], writes=["castb", "attT4", "l1_ybf"],
                      tag=name)
            prog.dma(dst_bf_ap[k * 128:(k + 1) * 128, c0:ce],
                     stb[:, 0:ce - c0],
                     reads=["castb"], writes=[name], queue="poolq", tag=name)


def stream_bf16_tiles(ctx, dram_bf_ap, k_tiles, tiles, tile_name, src_name,
                      queue="sp", tag="", width=None):
    prog = ctx.prog
    w = width if width is not None else dram_bf_ap.shape[-1]
    for k in range(k_tiles):
        prog.dma(tiles[k][0:128, 0:w], dram_bf_ap[k * 128:(k + 1) * 128, :],
                 reads=[src_name], writes=[tile_name], queue=queue, tag=tag)
    return tiles



def emit_dense_mm(ctx, w_tiles, w_buf, src_bf, src_bufs, n_out_tiles,
                  consume, tag):
    nc, prog = ctx.nc, ctx.prog
    nk = len(src_bf)
    for o in range(n_out_tiles):
        for half in range(2):
            bank = 0 if half == 0 else 2
            pname = f"ps{bank}"
            pm = ctx.psb(bank, 2)

            def mm(h, o=o, half=half, pm=pm):
                last = None
                for k in range(nk):
                    for j in range(2):
                        c0 = half * 1024 + j * 512
                        last = nc.tensor.matmul(
                            pm[:, j * 512:(j + 1) * 512],
                            w_tiles[k][:, o * 128:(o + 1) * 128],
                            src_bf[k][:, c0:c0 + 512],
                            start=(k == 0), stop=(k == nk - 1))
                return last
            prog.step("pe", mm, reads=list(src_bufs) + [w_buf],
                      writes=[pname, f"ps{bank+1}"], tag=tag + f"mm{o}.{half}")
            consume(o, half, pname, pm)


def emit_l2_ffn(ctx, x_bufs, xT, call_id, w_dram):
    nc, prog = ctx.nc, ctx.prog
    tag = f"l2f{call_id}_"
    ln_bf = [ctx.sb[f"bfA{i}"] for i in range(NDT)]
    ln_bufs = _bufnames("bfA", NDT)
    emit_layernorm(ctx, x_bufs, xT, ln_bf, ln_bufs, tag)

    h_bf = [ctx.sb[f"bfB{i}"] for i in range(NDT)]
    h_bufs = _bufnames("bfB", NDT)

    def evac_to(dst_bf, dst_bufs, stag):
        def consume(o, half, pname, pm):
            def ev(h, o=o, half=half, pm=pm):
                return nc.vector.tensor_copy(
                    out=dst_bf[o][:, half * 1024:(half + 1) * 1024], in_=pm[:])
            prog.step("dve", ev, reads=[pname],
                      writes=[dst_bufs[o], pname], tag=stag)
        return consume

    emit_dense_mm(ctx, ctx.sb_w["l2_wW"], "l2_wW", ln_bf, ln_bufs, NDT,
                  evac_to(h_bf, h_bufs, tag + "s0"), tag + "s0")
    load_weight_bf16(ctx, "l2_wA2", w_dram["l2_wA2"], NDT, D,
                     reuse=(ctx.sb_w["l2_wW"], "l2_wW"))
    emit_dense_mm(ctx, ctx.sb_w["l2_wA1"], "l2_wA1", h_bf, h_bufs, NDT,
                  evac_to(ln_bf, ln_bufs, tag + "s1"), tag + "s1")

    scr = ctx.sb["scrA"]

    def consume_final(o, half, pname, pm):
        def ev(h, o=o, half=half, pm=pm):
            return nc.scalar.activation(
                out=scr[:, half * 1024:(half + 1) * 1024],
                in_=pm[:], func=AF.Tanh,
                bias=ctx.sb["l2_bias"][:, o:o + 1])
        prog.step("act", ev, reads=[pname, "l2_bias"],
                  writes=["scrA", pname], tag=tag + "s2")
        if half == 1:
            def addres(h, o=o):
                return nc.vector.scalar_tensor_tensor(
                    out=xT[o][:], in0=scr[:], scalar=0.5, in1=xT[o][:],
                    op0=ALU.mult, op1=ALU.add)
            prog.step("dve", addres, reads=["scrA", x_bufs[o]],
                      writes=[x_bufs[o]], tag=tag + "res")

    emit_dense_mm(ctx, ctx.sb_w["l2_wA2"], "l2_wA2", ln_bf, ln_bufs, NDT,
                  consume_final, tag + "s2")



def emit_moe_gate(ctx, ln_bf, ln_bufs, gwT_name, n_exp, top_k, tag):
    nc, prog = ctx.nc, ctx.prog
    scr = ctx.sb["scrA"]
    rows = ctx.sb["score_rows"]
    gwT_tiles = ctx.sb_w[gwT_name]
    E = n_exp
    ST = 96

    def sl(ti, a, b):
        return scr[:, ti * ST + a:ti * ST + b]

    for ti in range(NTT):
        pg = ctx.psb(4 + (ti % 2), 1)

        def mm(h, ti=ti, pg=pg):
            last = None
            for k in range(NDT):
                last = nc.tensor.matmul(
                    pg[:, 0:E], ln_bf[k][:, ti * 128:(ti + 1) * 128],
                    gwT_tiles[k][:, 0:E],
                    start=(k == 0), stop=(k == NDT - 1))
            return last
        prog.step("pe", mm, reads=list(ln_bufs) + [gwT_name],
                  writes=[f"ps{4 + (ti % 2)}"], tag=tag + f"g{ti}")
        prog.step("dve",
                  lambda h, ti=ti, pg=pg: nc.vector.tensor_copy(
                      out=sl(ti, 16, 32), in_=pg[:, 0:E]),
                  reads=[f"ps{4 + (ti % 2)}"],
                  writes=["scrA", f"ps{4 + (ti % 2)}"], tag=tag)

    def phase(eng, fn, name):
        for ti in range(NTT):
            prog.step(eng, lambda h, ti=ti: fn(ti),
                      reads=["scrA"], writes=["scrA"], tag=tag + name)

    phase("dve", lambda ti: nc.vector.reduce_max(
        sl(ti, 48, 49), sl(ti, 16, 32), axis=AX.X, negate=True), "negmax")
    phase("act", lambda ti: nc.scalar.activation(
        out=sl(ti, 0, 16), in_=sl(ti, 16, 32), func=AF.Exp,
        bias=sl(ti, 48, 49)), "exp")
    phase("dve", lambda ti: nc.vector.memset(sl(ti, 32, 48), 0.0), "mz")
    for r in range(top_k):
        phase("dve", lambda ti: nc.vector.reduce_max(
            sl(ti, 48, 49), sl(ti, 16, 32), axis=AX.X), f"max{r}")
        phase("dve", lambda ti: nc.vector.tensor_tensor(
            out=sl(ti, 64, 80), in0=sl(ti, 16, 32),
            in1=sl(ti, 48, 49).broadcast_to([128, E]), op=ALU.is_ge),
            f"sel{r}")
        phase("dve", lambda ti: nc.vector.tensor_add(
            sl(ti, 32, 48), sl(ti, 32, 48), sl(ti, 64, 80)), f"acc{r}")
        phase("dve", lambda ti: nc.vector.scalar_tensor_tensor(
            out=sl(ti, 16, 32), in0=sl(ti, 64, 80), scalar=-1e30,
            in1=sl(ti, 16, 32), op0=ALU.mult, op1=ALU.add), f"rm{r}")
    phase("dve", lambda ti: nc.vector.tensor_mul(
        sl(ti, 16, 32), sl(ti, 0, 16), sl(ti, 32, 48)), "pw")
    phase("dve", lambda ti: nc.vector.reduce_sum(
        sl(ti, 50, 51), sl(ti, 16, 32), axis=AX.X), "z")
    phase("dve", lambda ti: nc.vector.reciprocal(
        sl(ti, 50, 51), sl(ti, 50, 51)), "rz")
    phase("dve", lambda ti: nc.vector.tensor_scalar(
        out=sl(ti, 16, 32), in0=sl(ti, 16, 32), scalar1=sl(ti, 50, 51),
        scalar2=None, op0=ALU.mult), "nrm")

    for ti in range(NTT):
        ptr = ctx.psb(4 + (ti % 2), 1)

        def tr(h, ti=ti, ptr=ptr):
            return nc.tensor.transpose(
                ptr[0:E, 0:128], sl(ti, 16, 32), ctx.sb["ident"][:])
        prog.step("pe", tr, reads=["scrA", "ident"],
                  writes=[f"ps{4 + (ti % 2)}"], tag=tag + "tr")
        prog.step("dve",
                  lambda h, ti=ti, ptr=ptr: nc.vector.tensor_copy(
                      out=rows[0:E, ti * 128:(ti + 1) * 128],
                      in_=ptr[0:E, 0:128]),
                  reads=[f"ps{4 + (ti % 2)}"],
                  writes=["score_rows", f"ps{4 + (ti % 2)}"], tag=tag)


def emit_moe_experts(ctx, x_bufs, xT, ln_bf, ln_bufs, cfg, tag):
    nc, prog = ctx.nc, ctx.prog
    rows = ctx.sb["score_rows"]
    bcd = ctx.dram["bc_scr"]
    E = cfg["n_exp"]
    INTER = cfg["inter"]
    n_it = INTER // 128
    NCH = 4
    CW = TOK // NCH
    kt_per_e = INTER // 128
    total_kt = E * kt_per_e
    half_kt = total_kt // 2
    e_per_half = E // 2

    for c in range(NCH):
        for eh in range(2):
            for ei in range(e_per_half):
                e = eh * e_per_half + ei
                par = e % cfg.get("n_par", 2)
                w1t = ctx.sb_w[f"moe_w1_{par}"][:NDT]
                stream_bf16_tiles(ctx, cfg["w1_dram"][e], NDT, w1t,
                                  f"moe_w1_{par}", cfg["w1_src"], tag=tag)
                if cfg["use_w3"]:
                    w3t = ctx.sb_w[f"moe_w3_{par}"][:NDT]
                    stream_bf16_tiles(ctx, cfg["w3_dram"][e], NDT, w3t,
                                      f"moe_w3_{par}", cfg["w3_src"], tag=tag)
                prog.dma(bcd[2:3, 0:CW], rows[e:e + 1, c * CW:(c + 1) * CW],
                         reads=["score_rows"], writes=["bc2"], tag=tag)
                prog.dma(ctx.sb["score_b"][:, 0:CW],
                         bcd[2:3, 0:CW].broadcast_to([128, CW]),
                         reads=["bc2"], writes=["score_b"], tag=tag)
                for it in range(n_it):
                    pg = ctx.psb(4, 1)
                    pu = ctx.psb(5, 1)

                    def mm_gu(h, e=e, it=it, c=c, par=par):
                        w1t = ctx.sb_w[f"moe_w1_{par}"]
                        last = None
                        for k in range(NDT):
                            last = nc.tensor.matmul(
                                pg, w1t[k][:, it * 128:(it + 1) * 128],
                                ln_bf[k][:, c * CW:(c + 1) * CW],
                                start=(k == 0), stop=(k == NDT - 1))
                        if cfg["use_w3"]:
                            w3t = ctx.sb_w[f"moe_w3_{par}"]
                            for k in range(NDT):
                                last = nc.tensor.matmul(
                                    pu, w3t[k][:, it * 128:(it + 1) * 128],
                                    ln_bf[k][:, c * CW:(c + 1) * CW],
                                    start=(k == 0), stop=(k == NDT - 1))
                        return last
                    rd = list(ln_bufs) + [f"moe_w1_{par}"]
                    wr = ["ps4"]
                    if cfg["use_w3"]:
                        rd.append(f"moe_w3_{par}")
                        wr.append("ps5")
                    prog.step("pe", mm_gu, reads=rd, writes=wr,
                              tag=tag + f"gu{e}.{it}")

                    kidx = ei * n_it + it
                    hname = f"hall{kidx}"
                    cfg["glue"](ctx, e, it, c, CW, pg, pu,
                                ctx.sb[hname], hname, tag,
                                extra=cfg.get("extra_h", lambda k: [])(kidx))

            for k in range(half_kt):
                e = eh * e_per_half + k // kt_per_e
                kt = k % kt_per_e
                par2 = k % 2
                w2sl = cfg["w2_dram"][e][kt * 128:(kt + 1) * 128, :]
                prog.dma(ctx.sb_w[f"moe_w2_{par2}"][0][:], w2sl,
                         reads=[cfg["w2_src"]], writes=[f"moe_w2_{par2}"],
                         queue="sp", tag=tag)
                last_k = (k == half_kt - 1) and (
                    cfg.get("bias_mm") is None or eh == 0)

                def mm_d(h, k=k, par2=par2, last_k=last_k, eh=eh):
                    w2t = ctx.sb_w[f"moe_w2_{par2}"][0]
                    last = None
                    for o in range(NDT):
                        last = nc.tensor.matmul(
                            ctx.psb(o, 1),
                            w2t[:, o * 128:(o + 1) * 128],
                            ctx.sb[f"hall{k}"][:, 0:CW],
                            start=(k == 0), stop=last_k)
                    return last
                prog.step("pe", mm_d,
                          reads=[f"hall{k}", f"moe_w2_{par2}"] +
                                cfg.get("extra_h", lambda k: [])(k),
                          writes=[f"ps{o}" for o in range(NDT)],
                          tag=tag + f"d{c}.{eh}.{k}")

            if cfg.get("bias_mm") is not None:
                cfg["bias_mm"](ctx, c, CW, eh, tag)

            for o in range(NDT):
                def ev(h, o=o, c=c):
                    return nc.vector.scalar_tensor_tensor(
                        out=xT[o][:, c * CW:(c + 1) * CW],
                        in0=ctx.psb(o, 1), scalar=0.5,
                        in1=xT[o][:, c * CW:(c + 1) * CW],
                        op0=ALU.mult, op1=ALU.add)
                prog.step("dve", ev, reads=[f"ps{o}", x_bufs[o]],
                          writes=[x_bufs[o], f"ps{o}"], tag=tag + f"ev{c}")


def alloc_moe(ctx, inter_w2, w1_width=512, w3_width=512, use_w3=True,
              n_par=2, hall_from_sh=False):
    if hall_from_sh:
        for i in range(32):
            ctx.sb[f"hall{i}"] = ctx.sb[f"sh{i // 4}"][
                0:128, (i % 4) * 512:(i % 4 + 1) * 512]
    else:
        for i in range(32):
            ctx.sbuf(f"hall{i}", [128, 512], BF16)
    for par in range(n_par):
        ctx.sb_w[f"moe_w1_{par}"] = [
            ctx.sbuf(f"moe_w1_{par}_{k}", [128, w1_width], BF16)
            for k in range(NDT)]
        if use_w3:
            ctx.sb_w[f"moe_w3_{par}"] = [
                ctx.sbuf(f"moe_w3_{par}_{k}", [128, w3_width], BF16)
                for k in range(NDT)]
    for par in range(2):
        ctx.sb_w[f"moe_w2_{par}"] = [
            ctx.sbuf(f"moe_w2_{par}_0", [128, inter_w2], BF16)]
    ctx.sbuf("score_rows", [16, TOK], F32)
    ctx.sbuf("score_b", [128, 512], F32)
    ctx.sb["castf"] = ctx.sb["scrA"]
    ctx.sbuf("castb", [128, 1024], BF16)


def glue_l1(ctx, e, it, c, CW, pg, pu, ht, hname, tag, extra=()):
    nc, prog = ctx.nc, ctx.prog
    scr = ctx.sb["scrA"]
    prog.step("act",
              lambda h: nc.scalar.activation(
                  out=scr[:, 0:CW], in_=pg, func=AF.Silu),
              reads=["ps4"], writes=["scrA", "ps4"], tag=tag)

    def glue_b(h):
        nc.vector.tensor_mul(scr[:, CW:2 * CW], scr[:, 0:CW],
                             ctx.sb["score_b"][:, 0:CW])
        return nc.vector.tensor_tensor(
            out=ht[:, 0:CW], in0=scr[:, CW:2 * CW], in1=pu, op=ALU.mult)
    prog.step("dve", glue_b, reads=["scrA", "score_b", "ps5"],
              writes=[hname, "scrA", "ps5"] + list(extra), tag=tag)


def emit_l1_ffn(ctx, x_bufs, xT, call_id, wdr):
    nc, prog = ctx.nc, ctx.prog
    tag = f"l1f{call_id}_"
    ln_bf = [ctx.sb[f"bfA{i}"] for i in range(NDT)]
    ln_bufs = _bufnames("bfA", NDT)
    emit_layernorm(ctx, x_bufs, xT, ln_bf, ln_bufs, tag)
    emit_moe_gate(ctx, ln_bf, ln_bufs, "l1_gwT", 16, 4, tag)
    if ctx.dram.get("dbg_scr") is not None:
        ctx.prog.dma(ctx.dram["dbg_scr"], ctx.sb["scrA"][:, 0:128],
                     reads=["scrA"], writes=["dbg"])
    cfg = dict(n_exp=16, inter=512, use_w3=True, glue=glue_l1,
               w1_dram=wdr["l1_w1_bf"], w1_src="l1w1bf",
               w3_dram=wdr["l1_w3_bf"], w3_src="l1w3bf",
               w2_dram=wdr["l1_w2_bf"], w2_src="l1w2bf")
    emit_moe_experts(ctx, x_bufs, xT, ln_bf, ln_bufs, cfg, tag)



def _mk_base(nc):
    xin = nc.dram_tensor("xT", [D, TOK], F32, kind="ExternalInput").ap()
    consts_in = nc.dram_tensor("const_in", [128, len(CONST_COLS)], F32,
                               kind="ExternalInput").ap()
    ident_in = nc.dram_tensor("ident_in", [128, 128], F32,
                              kind="ExternalInput").ap()
    xout = nc.dram_tensor("xT_out", [D, TOK], F32, kind="ExternalOutput").ap()
    return xin, consts_in, ident_in, xout


def _setup_common(ctx, xin, consts_in, ident_in, want_ident=False):
    ctx.prog.dma(ctx.sb["consts"][:], consts_in, writes=["consts"])
    if want_ident:
        ctx.sbuf("ident", [128, 128], F32)
        ctx.prog.dma(ctx.sb["ident"][:], ident_in, writes=["ident"])
    load_stream_inputs(ctx, xin)


def build_l2_ffn_program():
    nc = bass.Bass("TRN2", target_bir_lowering=False)
    xin, consts_in, ident_in, xout = _mk_base(nc)
    w_in = {}
    for nm in ("l2_wW", "l2_wA1", "l2_wA2"):
        w_in[nm] = nc.dram_tensor(nm, [D, D], F32, kind="ExternalInput").ap()
    w_in["l2_bias"] = nc.dram_tensor("l2_bias_in", [128, NDT], F32,
                                     kind="ExternalInput").ap()
    with contextlib.ExitStack() as stack:
        prog = Prog(nc)
        ctx = Ctx(nc, prog, stack)
        alloc_common(ctx)
        _setup_common(ctx, xin, consts_in, ident_in)
        for nm in ("l2_wW", "l2_wA1"):
            load_weight_bf16(ctx, nm, w_in[nm], NDT, D)
        ctx.sbuf("l2_bias", [128, NDT], F32)
        prog.dma(ctx.sb["l2_bias"][:], w_in["l2_bias"], writes=["l2_bias"])
        xT = [ctx.sb[f"xT{i}"] for i in range(NDT)]
        emit_l2_ffn(ctx, _bufnames("xT", NDT), xT, 0,
                    {"l2_wA2": w_in["l2_wA2"]})
        store_stream_outputs(ctx, xout)
        prog.finalize(stack)
    return nc


def build_l1_ffn_program(debug=False):
    nc = bass.Bass("TRN2", target_bir_lowering=False)
    xin, consts_in, ident_in, xout = _mk_base(nc)
    w_in = {
        "l1_gwT": nc.dram_tensor("l1_gwT", [D, 16], F32,
                                 kind="ExternalInput").ap(),
        "l1_w1": nc.dram_tensor("l1_w1", [16, D, 512], F32,
                                kind="ExternalInput").ap(),
        "l1_w3": nc.dram_tensor("l1_w3", [16, D, 512], F32,
                                kind="ExternalInput").ap(),
        "l1_w2": nc.dram_tensor("l1_w2", [16, 512, D], F32,
                                kind="ExternalInput").ap(),
    }
    with contextlib.ExitStack() as stack:
        prog = Prog(nc)
        ctx = Ctx(nc, prog, stack)
        alloc_common(ctx, bf_groups=1)
        alloc_moe(ctx, inter_w2=D)
        _setup_common(ctx, xin, consts_in, ident_in, want_ident=True)
        load_weight_bf16(ctx, "l1_gwT", w_in["l1_gwT"], NDT, 16)

        wdr = {}
        w1bf = nc.dram_tensor("l1_w1_bf", [16, D, 512], BF16, kind="Internal").ap()
        w3bf = nc.dram_tensor("l1_w3_bf", [16, D, 512], BF16, kind="Internal").ap()
        w2bf = nc.dram_tensor("l1_w2_bf", [16, 512, D], BF16, kind="Internal").ap()
        for e in range(16):
            precast_to_dram(ctx, "l1w1bf", w_in["l1_w1"][e], w1bf[e], D, 512)
            precast_to_dram(ctx, "l1w3bf", w_in["l1_w3"][e], w3bf[e], D, 512)
            precast_to_dram(ctx, "l1w2bf", w_in["l1_w2"][e], w2bf[e], 512, D)
        wdr["l1_w1_bf"] = w1bf
        wdr["l1_w3_bf"] = w3bf
        wdr["l1_w2_bf"] = w2bf

        if debug:
            ctx.dram["dbg_scr"] = nc.dram_tensor(
                "dbg_scr", [128, 128], F32, kind="ExternalOutput").ap()
        xT = [ctx.sb[f"xT{i}"] for i in range(NDT)]
        emit_l1_ffn(ctx, _bufnames("xT", NDT), xT, 0, wdr)
        store_stream_outputs(ctx, xout)
        if debug:
            dbg = nc.dram_tensor("dbg_scores", [16, TOK], F32,
                                 kind="ExternalOutput").ap()
            prog.dma(dbg, ctx.sb["score_rows"][:], reads=["score_rows"])
            dbg2 = nc.dram_tensor("dbg_hall", [128, 512], F32,
                                  kind="ExternalOutput").ap()
            prog.step("dve", lambda h: nc.vector.tensor_copy(
                out=ctx.sb["scrA"][:, 0:512], in_=ctx.sb["hall0"][:]),
                reads=["hall0"], writes=["scrA"])
            prog.dma(dbg2, ctx.sb["scrA"][:, 0:512], reads=["scrA"])
        prog.finalize(stack)
    return nc


def stage_l2_ffn_weights(inp):
    out = {}
    for nm, key in (("l2_wW", "L2_ffn_weight"), ("l2_wA1", "L2_ffn_aux_weight1"),
                    ("l2_wA2", "L2_ffn_aux_weight2")):
        out[nm] = np.ascontiguousarray(np.asarray(inp[key], dtype=np.float32))
    out["l2_bias"] = np.ascontiguousarray(
        np.asarray(inp["L2_ffn_bias"], dtype=np.float32).reshape(NDT, 128).T)
    return out


def stage_l1_ffn_weights(inp):
    return {
        "l1_gwT": np.ascontiguousarray(
            np.asarray(inp["L1_ffn_gate_w"], np.float32).T),
        "l1_w1": np.ascontiguousarray(np.asarray(inp["L1_ffn_w1"], np.float32)),
        "l1_w3": np.ascontiguousarray(np.asarray(inp["L1_ffn_w3"], np.float32)),
        "l1_w2": np.ascontiguousarray(np.asarray(inp["L1_ffn_w2"], np.float32)),
    }


def _run_program(nc, x_by_core, weights_np, key_renames=()):
    consts = make_const_host()
    ident = make_ident_host()
    ren = dict(key_renames)
    in_maps = []
    for c in range(8):
        m = dict(xT=x_by_core[c], const_in=consts, ident_in=ident)
        for k, v in weights_np.items():
            m[ren.get(k, k)] = v
        in_maps.append(m)
    res = run_bass_kernel_spmd(nc, in_maps, core_ids=list(range(8)))
    return [np.asarray(r["xT_out"]) for r in res.results]


def run_l2_ffn(x_by_core, weights_np):
    nc = build_l2_ffn_program()
    return _run_program(nc, x_by_core, weights_np,
                        key_renames=[("l2_bias", "l2_bias_in")])


def run_l1_ffn(x_by_core, weights_np, debug=False):
    nc = build_l1_ffn_program(debug=debug)
    consts = make_const_host()
    ident = make_ident_host()
    in_maps = []
    for c in range(8):
        m = dict(xT=x_by_core[c], const_in=consts, ident_in=ident)
        m.update(weights_np)
        in_maps.append(m)
    res = run_bass_kernel_spmd(nc, in_maps, core_ids=list(range(8)))
    return res.results





def glue_l3(ctx, e, it, c, CW, pg, pu, ht, hname, tag, extra=()):
    nc, prog = ctx.nc, ctx.prog
    scr = ctx.sb["scrA"]
    prog.step("act",
              lambda h: nc.scalar.activation(
                  out=scr[:, 0:CW], in_=pg, func=AF.Silu),
              reads=["ps4"], writes=["scrA", "ps4"], tag=tag)
    prog.step("dve",
              lambda h: nc.vector.tensor_tensor(
                  out=ht[:, 0:CW], in0=scr[:, 0:CW],
                  in1=ctx.sb["score_b"][:, 0:CW], op=ALU.mult),
              reads=["scrA", "score_b"],
              writes=[hname, "scrA"] + list(extra), tag=tag)


def emit_l3_shared(ctx, x_bufs, xT, ln_bf, ln_bufs, wdr, tag):
    nc, prog = ctx.nc, ctx.prog
    scr = ctx.sb["scrA"]
    PS = ctx.sb["PS"]
    CW = 256
    sh_tmp = ctx.sb["sh_tmp"]
    for c in range(8):
        for s in range(16):
            par = 0
            for wname in ("sfc1", "sfc2"):
                dst = ctx.sb_w[f"sh{wname}_{par}"]
                src_ap = wdr[f"{wname}T_bf"][s]
                for k in range(NDT):
                    prog.dma(dst[k][:], src_ap[k * 128:(k + 1) * 128, :],
                             reads=[f"{wname}Tbf"], writes=[f"sh{wname}_{par}"],
                             queue="sp", tag=tag)
            pg = ctx.psb(4, 1)[:, 0:CW]
            pu = ctx.psb(5, 1)[:, 0:CW]

            def mm_gu(h, s=s, c=c, par=par):
                g1 = ctx.sb_w[f"shsfc1_{par}"]
                g2 = ctx.sb_w[f"shsfc2_{par}"]
                last = None
                for k in range(NDT):
                    nc.tensor.matmul(
                        pg, g1[k][:], ln_bf[k][:, c * CW:(c + 1) * CW],
                        start=(k == 0), stop=(k == NDT - 1))
                    last = nc.tensor.matmul(
                        pu, g2[k][:], ln_bf[k][:, c * CW:(c + 1) * CW],
                        start=(k == 0), stop=(k == NDT - 1))
                return last
            prog.step("pe", mm_gu,
                      reads=list(ln_bufs) + [f"shsfc1_{par}", f"shsfc2_{par}"],
                      writes=["ps4", "ps5"], tag=tag + f"sh{c}.{s}")
            prog.step("act",
                      lambda h, pg=pg: nc.scalar.activation(
                          out=scr[:, 0:CW], in_=pg, func=AF.Silu),
                      reads=["ps4"], writes=["scrA", "ps4"], tag=tag)
            prog.step("dve",
                      lambda h, pu=pu: nc.vector.tensor_tensor(
                          out=sh_tmp[:, 0:CW], in0=scr[:, 0:CW], in1=pu,
                          op=ALU.mult),
                      reads=["scrA", "ps5"],
                      writes=["sh_tmp", "scrA", "ps5"], tag=tag)

            par2 = s % 2
            prog.dma(ctx.sb_w[f"sfc3_{par2}"][0][:],
                     wdr["sfc3T_bf"][s * 128:(s + 1) * 128, :],
                     reads=["sfc3Tbf"], writes=[f"sfc3_{par2}"],
                     queue="sp", tag=tag)

            def mm_d(h, s=s, par2=par2):
                w2t = ctx.sb_w[f"sfc3_{par2}"][0]
                last = None
                for o in range(NDT):
                    last = nc.tensor.matmul(
                        PS[:, o * CW:(o + 1) * CW],
                        w2t[:, o * 128:(o + 1) * 128],
                        sh_tmp[:, 0:CW],
                        start=(s == 0 and o % 2 == 0), stop=(s == 15))
                return last
            prog.step("pe", mm_d, reads=["sh_tmp", f"sfc3_{par2}"],
                      writes=["ps0", "ps1", "ps2", "ps3"],
                      tag=tag + f"shd{c}.{s}")
        for o in range(NDT):
            def ev(h, o=o, c=c):
                return nc.vector.scalar_tensor_tensor(
                    out=xT[o][:, c * CW:(c + 1) * CW],
                    in0=PS[:, o * CW:(o + 1) * CW], scalar=0.5,
                    in1=xT[o][:, c * CW:(c + 1) * CW],
                    op0=ALU.mult, op1=ALU.add)
            prog.step("dve", ev,
                      reads=["ps0", "ps1", "ps2", "ps3", x_bufs[o]],
                      writes=[x_bufs[o], "ps0", "ps1", "ps2", "ps3"],
                      tag=tag + f"shev{c}")


def emit_l3_ffn(ctx, x_bufs, xT, call_id, wdr):
    nc, prog = ctx.nc, ctx.prog
    tag = f"l3f{call_id}_"
    ln_bf = [ctx.sb[f"bfA{i}"] for i in range(NDT)]
    ln_bufs = _bufnames("bfA", NDT)
    emit_layernorm(ctx, x_bufs, xT, ln_bf, ln_bufs, tag)
    emit_moe_gate(ctx, ln_bf, ln_bufs, "l3_gwT", 16, 2, tag)
    emit_l3_shared(ctx, x_bufs, xT, ln_bf, ln_bufs, wdr, tag)
    if ctx.dram.get("dbg_mid") is not None:
        for i in range(NDT):
            ctx.prog.dma(ctx.dram["dbg_mid"][i * 128:(i + 1) * 128, :],
                         xT[i][:], reads=[x_bufs[i]], writes=["dbgm"])
    cfg = dict(n_exp=16, inter=512, use_w3=False, glue=glue_l3, n_par=1,
               w1_dram=wdr["w1T_bf"], w1_src="l3w1Tbf",
               w2_dram=wdr["w2T_bf"], w2_src="l3w2Tbf")
    emit_moe_experts(ctx, x_bufs, xT, ln_bf, ln_bufs, cfg, tag)


def build_l3_ffn_program(debug=False):
    nc = bass.Bass("TRN2", target_bir_lowering=False)
    xin, consts_in, ident_in, xout = _mk_base(nc)
    w_in = {
        "l3_gwT": nc.dram_tensor("l3_gwT", [D, 16], F32,
                                 kind="ExternalInput").ap(),
        "l3_w1T": nc.dram_tensor("l3_w1T", [16, D, 512], F32,
                                 kind="ExternalInput").ap(),
        "l3_w2T": nc.dram_tensor("l3_w2T", [16, 512, D], F32,
                                 kind="ExternalInput").ap(),
        "l3_sfc1T": nc.dram_tensor("l3_sfc1T", [16, D, 128], F32,
                                   kind="ExternalInput").ap(),
        "l3_sfc2T": nc.dram_tensor("l3_sfc2T", [16, D, 128], F32,
                                   kind="ExternalInput").ap(),
        "l3_sfc3T": nc.dram_tensor("l3_sfc3T", [2048, D], F32,
                                   kind="ExternalInput").ap(),
    }
    with contextlib.ExitStack() as stack:
        prog = Prog(nc)
        ctx = Ctx(nc, prog, stack)
        alloc_common(ctx, bf_groups=1)
        ctx.sbuf("sh_tmp", [128, 256], BF16)
        alloc_moe(ctx, inter_w2=D, w1_width=512, use_w3=False, n_par=1)
        if debug:
            ctx.dram["dbg_mid"] = nc.dram_tensor(
                "dbg_mid", [D, TOK], F32, kind="ExternalOutput").ap()
        for par in range(2):
            ctx.sb_w[f"shsfc1_{par}"] = [
                ctx.sbuf(f"shsfc1_{par}_{k}", [128, 128], BF16)
                for k in range(NDT)]
            ctx.sb_w[f"shsfc2_{par}"] = [
                ctx.sbuf(f"shsfc2_{par}_{k}", [128, 128], BF16)
                for k in range(NDT)]
            ctx.sb_w[f"sfc3_{par}"] = ctx.sb_w[f"moe_w2_{par}"]
        _setup_common(ctx, xin, consts_in, ident_in, want_ident=True)
        load_weight_bf16(ctx, "l3_gwT", w_in["l3_gwT"], NDT, 16)

        wdr = {}
        w1bf = nc.dram_tensor("l3_w1T_bf", [16, D, 512], BF16,
                              kind="Internal").ap()
        w2bf = nc.dram_tensor("l3_w2T_bf", [16, 512, D], BF16,
                              kind="Internal").ap()
        s1bf = nc.dram_tensor("l3_sfc1T_bf", [16, D, 128], BF16,
                              kind="Internal").ap()
        s2bf = nc.dram_tensor("l3_sfc2T_bf", [16, D, 128], BF16,
                              kind="Internal").ap()
        s3bf = nc.dram_tensor("l3_sfc3T_bf", [2048, D], BF16,
                              kind="Internal").ap()
        for e in range(16):
            precast_to_dram(ctx, "l3w1Tbf", w_in["l3_w1T"][e], w1bf[e], D, 512)
            precast_to_dram(ctx, "l3w2Tbf", w_in["l3_w2T"][e], w2bf[e], 512, D)
            precast_to_dram(ctx, "sfc1Tbf", w_in["l3_sfc1T"][e], s1bf[e],
                            D, 128)
            precast_to_dram(ctx, "sfc2Tbf", w_in["l3_sfc2T"][e], s2bf[e],
                            D, 128)
        precast_to_dram(ctx, "sfc3Tbf", w_in["l3_sfc3T"], s3bf, 2048, D)
        wdr = {"w1T_bf": w1bf, "w2T_bf": w2bf, "sfc1T_bf": s1bf,
               "sfc2T_bf": s2bf, "sfc3T_bf": s3bf}

        xT = [ctx.sb[f"xT{i}"] for i in range(NDT)]
        emit_l3_ffn(ctx, _bufnames("xT", NDT), xT, 0, wdr)
        store_stream_outputs(ctx, xout)
        if debug:
            dbg = nc.dram_tensor("dbg_scores", [16, TOK], F32,
                                 kind="ExternalOutput").ap()
            prog.dma(dbg, ctx.sb["score_rows"][:], reads=["score_rows"])
        prog.finalize(stack)
    return nc


def stage_l3_ffn_weights(inp):
    w1 = np.asarray(inp["L3_ffn_w1"], np.float32)
    w2 = np.asarray(inp["L3_ffn_w2"], np.float32)
    sfc1 = np.asarray(inp["L3_ffn_sfc1"], np.float32)
    sfc2 = np.asarray(inp["L3_ffn_sfc2"], np.float32)
    sfc3 = np.asarray(inp["L3_ffn_sfc3"], np.float32)
    return {
        "l3_gwT": np.ascontiguousarray(
            np.asarray(inp["L3_ffn_gate_w"], np.float32).T),
        "l3_w1T": np.ascontiguousarray(w1.transpose(0, 2, 1)),
        "l3_w2T": np.ascontiguousarray(w2.transpose(0, 2, 1)),
        "l3_sfc1T": np.ascontiguousarray(
            sfc1.T.reshape(D, 16, 128).transpose(1, 0, 2)),
        "l3_sfc2T": np.ascontiguousarray(
            sfc2.T.reshape(D, 16, 128).transpose(1, 0, 2)),
        "l3_sfc3T": np.ascontiguousarray(sfc3.T),
    }


def run_l3_ffn(x_by_core, weights_np, debug=False):
    nc = build_l3_ffn_program(debug=debug)
    consts = make_const_host()
    ident = make_ident_host()
    in_maps = []
    for c in range(8):
        m = dict(xT=x_by_core[c], const_in=consts, ident_in=ident)
        m.update(weights_np)
        in_maps.append(m)
    res = run_bass_kernel_spmd(nc, in_maps, core_ids=list(range(8)))
    return res.results



def emit_l0_scores(ctx, tag):
    nc, prog = ctx.nc, ctx.prog
    scr = ctx.sb["scrA"]
    rows = ctx.sb["score_rows"]
    idxf = ctx.sb["l0_idxf"]
    rwt = ctx.sb["l0_rwt"]
    ST = 32

    def sl(ti, a, b):
        return scr[:, ti * ST + a:ti * ST + b]

    def phase(eng, fn, name):
        for ti in range(NTT):
            prog.step(eng, lambda h, ti=ti: fn(ti),
                      reads=["scrA", "l0_idxf", "l0_rwt"], writes=["scrA"],
                      tag=tag + name)

    for e in range(8):
        phase("dve", lambda ti, e=e: nc.vector.tensor_single_scalar(
            out=sl(ti, 16, 18), in_=idxf[:, ti * 2:(ti + 1) * 2],
            scalar=float(e), op=ALU.is_equal), f"eq{e}")
        phase("dve", lambda ti, e=e: nc.vector.reduce_sum(
            sl(ti, 24, 25), sl(ti, 16, 18), axis=AX.X), f"cnt{e}")
        phase("dve", lambda ti, e=e: nc.vector.tensor_mul(
            sl(ti, e, e + 1), sl(ti, 24, 25),
            rwt[:, ti * 8 + e:ti * 8 + e + 1]), f"sc{e}")

    for ti in range(NTT):
        ptr = ctx.psb(4 + (ti % 2), 1)

        def tr(h, ti=ti, ptr=ptr):
            return nc.tensor.transpose(
                ptr[0:8, 0:128], sl(ti, 0, 8), ctx.sb["ident"][:])
        prog.step("pe", tr, reads=["scrA", "ident"],
                  writes=[f"ps{4 + (ti % 2)}"], tag=tag + "tr")
        prog.step("dve",
                  lambda h, ti=ti, ptr=ptr: nc.vector.tensor_copy(
                      out=rows[0:8, ti * 128:(ti + 1) * 128],
                      in_=ptr[0:8, 0:128]),
                  reads=[f"ps{4 + (ti % 2)}"],
                  writes=["score_rows", f"ps{4 + (ti % 2)}"], tag=tag)


def glue_l0(ctx, e, it, c, CW, pg, pu, ht, hname, tag, extra=()):
    nc, prog = ctx.nc, ctx.prog
    scr = ctx.sb["scrA"]
    bg = ctx.sb["l0_bg"][:, e * 8 + it:e * 8 + it + 1]
    bu = ctx.sb["l0_bu"][:, e * 8 + it:e * 8 + it + 1]

    def p1(h):
        return nc.vector.tensor_scalar(
            out=scr[:, 0:CW], in0=pg, scalar1=bg, scalar2=7.0,
            op0=ALU.add, op1=ALU.min)
    prog.step("dve", p1, reads=["ps4", "l0_bg"], writes=["scrA", "ps4"],
              tag=tag)
    prog.step("act",
              lambda h: nc.scalar.activation(
                  out=scr[:, CW:2 * CW], in_=scr[:, 0:CW], func=AF.Sigmoid,
                  scale=1.702),
              reads=["scrA"], writes=["scrA"], tag=tag)

    def p3(h):
        nc.vector.tensor_mul(scr[:, 0:CW], scr[:, 0:CW], scr[:, CW:2 * CW])
        return nc.vector.tensor_mul(scr[:, 0:CW], scr[:, 0:CW],
                                    ctx.sb["score_b"][:, 0:CW])
    prog.step("dve", p3, reads=["scrA", "score_b"], writes=["scrA"], tag=tag)

    def p4(h):
        nc.vector.tensor_scalar(
            out=scr[:, CW:2 * CW], in0=pu, scalar1=bu, scalar2=-7.0,
            op0=ALU.add, op1=ALU.max)
        nc.vector.tensor_scalar(
            out=scr[:, CW:2 * CW], in0=scr[:, CW:2 * CW], scalar1=7.0,
            scalar2=1.0, op0=ALU.min, op1=ALU.add)
        return nc.vector.tensor_tensor(
            out=ht[:, 0:CW], in0=scr[:, CW:2 * CW], in1=scr[:, 0:CW],
            op=ALU.mult)
    prog.step("dve", p4, reads=["scrA", "ps5", "l0_bu"],
              writes=[hname, "scrA", "ps5"] + list(extra), tag=tag)


def emit_l0_ffn(ctx, x_bufs, xT, call_id, wdr):
    nc, prog = ctx.nc, ctx.prog
    tag = f"l0f{call_id}_"
    ln_bf = [ctx.sb[f"bfA{i}"] for i in range(NDT)]
    ln_bufs = _bufnames("bfA", NDT)
    emit_layernorm(ctx, x_bufs, xT, ln_bf, ln_bufs, tag)
    emit_l0_scores(ctx, tag)

    def bias_mm(ctx2, c, CW, eh, tag2):
        if eh != 1:
            return
        db = ctx.sb["l0_db"]
        sc64 = ctx.sb.get("l0_sc64")
        if sc64 is not None:
            prog.dma(sc64, ctx.sb["score_rows"][0:8, c * CW:(c + 1) * CW],
                     reads=["score_rows"], writes=["l0_sc64"], queue="sp",
                     tag=tag2)
            rhs = sc64
            rdep = "l0_sc64"
        else:
            rhs = ctx.sb["score_rows"][0:8, c * CW:(c + 1) * CW]
            rdep = "score_rows"

        def mm(h, c=c, rhs=rhs):
            last = None
            for o in range(NDT):
                last = nc.tensor.matmul(
                    ctx.psb(o, 1), db[0:8, o * 128:(o + 1) * 128],
                    rhs, start=False, stop=True)
            return last
        prog.step("pe", mm, reads=["l0_db", rdep],
                  writes=[f"ps{o}" for o in range(NDT)], tag=tag2 + "bias")

    cfg = dict(n_exp=8, inter=1024, use_w3=True, glue=glue_l0, n_par=1,
               w1_dram=wdr["wg_bf"], w1_src="l0wgbf",
               w3_dram=wdr["wu_bf"], w3_src="l0wubf",
               w2_dram=wdr["wd_bf"], w2_src="l0wdbf",
               bias_mm=bias_mm)
    emit_moe_experts(ctx, x_bufs, xT, ln_bf, ln_bufs, cfg, tag)


def build_l0_ffn_program():
    nc = bass.Bass("TRN2", target_bir_lowering=False)
    xin, consts_in, ident_in, xout = _mk_base(nc)
    w_in = {
        "l0_wg": nc.dram_tensor("l0_wg", [8, D, D], F32,
                                kind="ExternalInput").ap(),
        "l0_wu": nc.dram_tensor("l0_wu", [8, D, D], F32,
                                kind="ExternalInput").ap(),
        "l0_wd": nc.dram_tensor("l0_wd", [8, D, D], F32,
                                kind="ExternalInput").ap(),
        "l0_bg": nc.dram_tensor("l0_bg_in", [128, 64], F32,
                                kind="ExternalInput").ap(),
        "l0_bu": nc.dram_tensor("l0_bu_in", [128, 64], F32,
                                kind="ExternalInput").ap(),
        "l0_db": nc.dram_tensor("l0_db_in", [8, D], F32,
                                kind="ExternalInput").ap(),
        "l0_idx": nc.dram_tensor("l0_idx", [TOK, 2], I32,
                                 kind="ExternalInput").ap(),
        "l0_rw": nc.dram_tensor("l0_rw", [TOK, 8], F32,
                                kind="ExternalInput").ap(),
    }
    with contextlib.ExitStack() as stack:
        prog = Prog(nc)
        ctx = Ctx(nc, prog, stack)
        alloc_common(ctx, bf_groups=1)
        alloc_moe(ctx, inter_w2=D, w1_width=D, w3_width=D, n_par=1)
        ctx.sbuf("l0_bg", [128, 64], F32)
        ctx.sbuf("l0_bu", [128, 64], F32)
        ctx.sbuf("l0_db", [8, D], F32)
        ctx.sbuf("l0_idxi", [128, 32], I32)
        ctx.sbuf("l0_idxf", [128, 32], F32)
        ctx.sbuf("l0_rwt", [128, 128], F32)
        _setup_common(ctx, xin, consts_in, ident_in, want_ident=True)
        prog.dma(ctx.sb["l0_bg"][:], w_in["l0_bg"], writes=["l0_bg"])
        prog.dma(ctx.sb["l0_bu"][:], w_in["l0_bu"], writes=["l0_bu"])
        prog.dma(ctx.sb["l0_db"][:], w_in["l0_db"], writes=["l0_db"])
        for ti in range(NTT):
            prog.dma(ctx.sb["l0_idxi"][:, ti * 2:(ti + 1) * 2],
                     w_in["l0_idx"][ti * 128:(ti + 1) * 128, :],
                     writes=["l0_idxi"])
            prog.dma(ctx.sb["l0_rwt"][:, ti * 8:(ti + 1) * 8],
                     w_in["l0_rw"][ti * 128:(ti + 1) * 128, :],
                     writes=["l0_rwt"])
        prog.step("pool", lambda h: nc.gpsimd.tensor_copy(
            out=ctx.sb["l0_idxf"][:], in_=ctx.sb["l0_idxi"][:]),
            reads=["l0_idxi"], writes=["l0_idxf"])

        wg_bf = nc.dram_tensor("l0_wg_bf", [8, D, D], BF16, kind="Internal").ap()
        wu_bf = nc.dram_tensor("l0_wu_bf", [8, D, D], BF16, kind="Internal").ap()
        wd_bf = nc.dram_tensor("l0_wd_bf", [8, D, D], BF16, kind="Internal").ap()
        for e in range(8):
            precast_to_dram(ctx, "l0wgbf", w_in["l0_wg"][e], wg_bf[e], D, D)
            precast_to_dram(ctx, "l0wubf", w_in["l0_wu"][e], wu_bf[e], D, D)
            precast_to_dram(ctx, "l0wdbf", w_in["l0_wd"][e], wd_bf[e], D, D)
        wdr = {"wg_bf": wg_bf, "wu_bf": wu_bf, "wd_bf": wd_bf}

        xT = [ctx.sb[f"xT{i}"] for i in range(NDT)]
        emit_l0_ffn(ctx, _bufnames("xT", NDT), xT, 0, wdr)
        store_stream_outputs(ctx, xout)
        prog.finalize(stack)
    return nc


def stage_l0_ffn_weights(inp):
    gup = np.asarray(inp["L0_ffn_gate_up_proj"], np.float32)
    gub = np.asarray(inp["L0_ffn_gate_up_proj_bias"], np.float32)
    dp = np.asarray(inp["L0_ffn_down_proj"], np.float32)
    dpb = np.asarray(inp["L0_ffn_down_proj_bias"], np.float32)
    wg = gup[:, :, 0::2]
    wu = gup[:, :, 1::2]
    bg = gub[:, 0::2]
    bu = gub[:, 1::2]
    bg_t = np.ascontiguousarray(
        bg.reshape(8, 8, 128).transpose(2, 0, 1).reshape(128, 64))
    bu_t = np.ascontiguousarray(
        bu.reshape(8, 8, 128).transpose(2, 0, 1).reshape(128, 64))
    return {
        "l0_wg": np.ascontiguousarray(wg),
        "l0_wu": np.ascontiguousarray(wu),
        "l0_wd": np.ascontiguousarray(dp),
        "l0_bg": bg_t, "l0_bu": bu_t,
        "l0_db": np.ascontiguousarray(dpb),
    }


def run_l0_ffn(x_by_core, weights_np, idx, rw):
    nc = build_l0_ffn_program()
    consts = make_const_host()
    ident = make_ident_host()
    zrw = np.zeros_like(rw)
    in_maps = []
    for c in range(8):
        m = dict(xT=x_by_core[c], const_in=consts, ident_in=ident,
                 l0_idx=idx, l0_rw=(rw if c == 0 else zrw))
        for k, v in weights_np.items():
            m[{"l0_bg": "l0_bg_in", "l0_bu": "l0_bu_in",
               "l0_db": "l0_db_in"}.get(k, k)] = v
        in_maps.append(m)
    res = run_bass_kernel_spmd(nc, in_maps, core_ids=list(range(8)))
    return res.results





def emit_mha(ctx, x_bufs, xT, tag, cfg):
    nc, prog = ctx.nc, ctx.prog
    ln_bf = [ctx.sb[f"bfA{i}"] for i in range(NDT)]
    ln_bufs = _bufnames("bfA", NDT)
    emit_layernorm(ctx, x_bufs, xT, ln_bf, ln_bufs, tag)

    scr = ctx.sb["scrA"]
    vf = ctx.sb["v_f32"]
    qbf, kbf = ctx.sb["q_bf"], ctx.sb["k_bf"]
    VT = [ctx.sb["VT0"], ctx.sb["VT1"]]
    attT = ctx.sb["attT4"]
    nmc = ctx.sb["nm_col"]
    PS = ctx.sb["PS"]

    for pair in range(8):
        par = 0
        for wi, wkey in enumerate(("q", "k", "v")):
            wt = ctx.sb_w["mha_ws"]
            dram = cfg[f"w{wkey}_bf"]
            for k in range(NDT):
                prog.dma(wt[k][:], dram[k * 128:(k + 1) * 128,
                                        pair * 128:(pair + 1) * 128],
                         reads=[cfg[f"w{wkey}_src"]],
                         writes=["mha_ws"], queue="sp", tag=tag)

            def mm(h, wt=wt):
                last = None
                for j in range(4):
                    for k in range(NDT):
                        last = nc.tensor.matmul(
                            PS[:, j * 512:(j + 1) * 512], wt[k][:],
                            ln_bf[k][:, j * 512:(j + 1) * 512],
                            start=(k == 0), stop=(k == NDT - 1))
                return last
            prog.step("pe", mm, reads=list(ln_bufs) + ["mha_ws"],
                      writes=["ps0", "ps1", "ps2", "ps3"],
                      tag=tag + f"p{pair}{wkey}")
            if wkey == "v":
                prog.step("dve",
                          lambda h: nc.vector.tensor_copy(
                              out=vf[:], in_=PS[:, 0:TOK]),
                          reads=["ps0", "ps1", "ps2", "ps3"],
                          writes=["v_f32", "ps0", "ps1", "ps2", "ps3"],
                          tag=tag)
            else:
                prog.step("dve",
                          lambda h: nc.vector.tensor_copy(
                              out=scr[:], in_=PS[:, 0:TOK]),
                          reads=["ps0", "ps1", "ps2", "ps3"],
                          writes=["scrA", "ps0", "ps1", "ps2", "ps3"],
                          tag=tag)
                cfg["norm_qk"](ctx, wkey, pair, qbf if wkey == "q" else kbf,
                               tag)

        for hh in range(2):
            base = hh * 64
            for g4 in range(4):
                def trs(h, hh=hh, g4=g4, base=base):
                    last = None
                    for u in range(4):
                        si = g4 * 4 + u
                        last = nc.tensor.transpose(
                            PS[:, u * 512:u * 512 + 64],
                            vf[base:base + 64, si * 128:(si + 1) * 128],
                            ctx.sb["ident"][base:base + 64, base:base + 64])
                    return last
                prog.step("pe", trs, reads=["v_f32", "ident"],
                          writes=["ps0", "ps1", "ps2", "ps3"],
                          tag=tag + "vt")
                def ev_v(h, hh=hh, g4=g4):
                    last = None
                    for u in range(4):
                        last = nc.vector.tensor_copy(
                            out=VT[hh][:, g4 * 256 + u * 64:
                                       g4 * 256 + (u + 1) * 64],
                            in_=PS[:, u * 512:u * 512 + 64])
                    return last
                prog.step("dve", ev_v,
                          reads=["ps0", "ps1", "ps2", "ps3"],
                          writes=[f"VT{hh}", "ps0", "ps1", "ps2", "ps3"],
                          tag=tag)

        for hh in range(2):
            h_abs = pair * 2 + hh
            base = hh * 64
            for qi in range(NTT):
                W = (qi + 1) * 128
                nch = (W + 511) // 512

                def mm_s(h, qi=qi, base=base, nch=nch, W=W):
                    last = None
                    for cc in range(nch):
                        w0 = cc * 512
                        cw = min(512, W - w0)
                        last = nc.tensor.matmul(
                            PS[:, w0:w0 + cw],
                            qbf[base:base + 64, qi * 128:(qi + 1) * 128],
                            kbf[base:base + 64, w0:w0 + cw],
                            start=True, stop=True)
                    return last
                prog.step("pe", mm_s, reads=["q_bf", "k_bf"],
                          writes=[f"ps{b}" for b in range(nch)],
                          tag=tag + f"s{h_abs}.{qi}")

                a0 = max(0, W - 512)
                aw = W - a0

                prog.step("dve",
                          lambda h, a0=a0, aw=aw: nc.vector.tensor_add(
                              PS[:, a0:a0 + aw], PS[:, a0:a0 + aw],
                              ctx.sb["mask4"][:, 512 - aw:512]),
                          reads=[f"ps{b}" for b in range(nch)] + ["mask4"],
                          writes=[f"ps{b}" for b in range(nch)],
                          tag=tag, sync_after=True)
                prog.step("dve",
                          lambda h, W=W: nc.vector.reduce_max(
                              nmc[:, 0:1], PS[:, 0:W], axis=AX.X, negate=True),
                          reads=[f"ps{b}" for b in range(nch)],
                          writes=["nm_col"], tag=tag)

                def soft_exp(h, qi=qi, W=W):
                    return nc.scalar.activation(
                        out=scr[:, 0:W], in_=PS[:, 0:W], func=AF.Exp,
                        bias=nmc[:, 0:1])
                prog.step("act", soft_exp,
                          reads=[f"ps{b}" for b in range(nch)] + ["nm_col"],
                          writes=["scrA"] + [f"ps{b}" for b in range(nch)],
                          tag=tag)

                prog.step("dve",
                          lambda h, W=W: nc.vector.reduce_sum(
                              nmc[:, 2:3], scr[:, 0:W], axis=AX.X),
                          reads=["scrA"], writes=["nm_col"], tag=tag,
                          sync_after=True)
                prog.step("dve",
                          lambda h: nc.vector.reciprocal(
                              nmc[:, 3:4], nmc[:, 2:3]),
                          reads=["nm_col"], writes=["nm_col"], tag=tag,
                          sync_after=True)
                prog.step("dve",
                          lambda h, W=W: nc.vector.tensor_scalar_mul(
                              scr[:, 0:W], scr[:, 0:W], nmc[:, 3:4]),
                          reads=["scrA", "nm_col"], writes=["scrA"], tag=tag)

                n_si = qi + 1
                for g4 in range((n_si + 3) // 4):
                    cnt = min(4, n_si - g4 * 4)

                    def trs(h, g4=g4, cnt=cnt):
                        last = None
                        for u in range(cnt):
                            si = g4 * 4 + u
                            last = nc.tensor.transpose(
                                PS[:, u * 512:u * 512 + 128],
                                scr[:, si * 128:(si + 1) * 128],
                                ctx.sb["ident"][:])
                        return last
                    prog.step("pe", trs, reads=["scrA", "ident"],
                              writes=["ps0", "ps1", "ps2", "ps3"],
                              tag=tag + "at")
                    def ev_t(h, cnt=cnt):
                        last = None
                        for u in range(cnt):
                            last = nc.vector.tensor_copy(
                                out=attT[:, u * 128:(u + 1) * 128],
                                in_=PS[:, u * 512:u * 512 + 128])
                        return last
                    prog.step("dve", ev_t,
                              reads=["ps0", "ps1", "ps2", "ps3"],
                              writes=["attT4", "ps0", "ps1", "ps2", "ps3"],
                              tag=tag)

                    def mm_av(h, g4=g4, cnt=cnt, hh=hh, qi=qi, n_si=n_si):
                        last = None
                        for u in range(cnt):
                            si = g4 * 4 + u
                            last = nc.tensor.matmul(
                                PS[0:64, 2560:2688],
                                VT[hh][:, si * 64:(si + 1) * 64],
                                attT[:, u * 128:(u + 1) * 128],
                                start=(si == 0), stop=(si == n_si - 1))
                        return last
                    prog.step("pe", mm_av,
                              reads=["attT4", f"VT{hh}"], writes=["ps5"],
                              tag=tag + f"av{h_abs}.{qi}")

                def ev_av(h, qi=qi, base=base, hh=hh, pair=pair):
                    return nc.vector.tensor_copy(
                        out=ctx.sb[f"aout{pair}"][base:base + 64,
                                                  qi * 128:(qi + 1) * 128],
                        in_=PS[0:64, 2560:2688])
                prog.step("dve", ev_av, reads=["ps5"],
                          writes=[f"aout{pair}", "ps5"], tag=tag)

    for c in range(4):
        for k in range(NDT):
            par2 = 0
            prog.dma(ctx.sb_w[f"mha_wo_{par2}"][0][:],
                     cfg["wo_bf"][k * 128:(k + 1) * 128, :],
                     reads=[cfg["wo_src"]], writes=[f"mha_wo_{par2}"],
                     queue="sp", tag=tag)

            def mm_o(h, k=k, c=c, par2=par2):
                wt = ctx.sb_w[f"mha_wo_{par2}"][0]
                last = None
                for o in range(NDT):
                    last = nc.tensor.matmul(
                        ctx.psb(o, 1),
                        wt[:, o * 128:(o + 1) * 128],
                        ctx.sb[f"aout{k}"][:, c * 512:(c + 1) * 512],
                        start=(k == 0), stop=(k == NDT - 1))
                return last
            prog.step("pe", mm_o, reads=[f"aout{k}", f"mha_wo_{par2}"],
                      writes=[f"ps{o}" for o in range(NDT)],
                      tag=tag + f"o{c}.{k}")
        for o in range(NDT):
            if cfg.get("bo_col") is not None:
                def ev(h, o=o, c=c):
                    nc.vector.tensor_scalar(
                        out=scr[:, 0:512], in0=ctx.psb(o, 1),
                        scalar1=cfg["bo_col"][:, o:o + 1], scalar2=None,
                        op0=ALU.add)
                    return nc.vector.tensor_add(
                        xT[o][:, c * 512:(c + 1) * 512],
                        xT[o][:, c * 512:(c + 1) * 512], scr[:, 0:512])
                prog.step("dve", ev,
                          reads=[f"ps{o}", x_bufs[o], "l0_bo", "scrA"],
                          writes=[x_bufs[o], f"ps{o}", "scrA"],
                          tag=tag + f"ev{c}")
            else:
                def ev(h, o=o, c=c):
                    return nc.vector.scalar_tensor_tensor(
                        out=xT[o][:, c * 512:(c + 1) * 512],
                        in0=ctx.psb(o, 1), scalar=1.0,
                        in1=xT[o][:, c * 512:(c + 1) * 512],
                        op0=ALU.mult, op1=ALU.add)
                prog.step("dve", ev, reads=[f"ps{o}", x_bufs[o]],
                          writes=[x_bufs[o], f"ps{o}"], tag=tag + f"ev{c}")


def _qk_stats(ctx, pair, tag, want_mu, sqrt_scale=None):
    nc, prog = ctx.nc, ctx.prog
    scr = ctx.sb["scrA"]
    vf = ctx.sb["v_f32"]
    ind = ctx.sb["ind2"]
    bcd = ctx.dram["bc_scr"]
    p6 = ctx.psb(6, 1, part=2)
    p7 = ctx.psb(7, 1, part=2)
    muv = vf[0:2, 512:1024]
    e2v = vf[0:2, 1024:1536]
    msq = vf[0:2, 1536:2048]
    epsc = ctx.sb["consts"]
    eps_col = epsc[0:2, CONST_IDX["eps_ln"]:CONST_IDX["eps_ln"] + 1] \
        if want_mu else epsc[0:2, CONST_IDX["eps_rms8"]:
                             CONST_IDX["eps_rms8"] + 1]

    for j in range(4):
        prog.step("act",
                  lambda h, j=j: nc.scalar.activation(
                      out=vf[:, 0:512], in_=scr[:, j * 512:(j + 1) * 512],
                      func=AF.Square),
                  reads=["scrA", "v_f32"], writes=["v_f32"], tag=tag)

        def mm(h, j=j):
            if want_mu:
                nc.tensor.matmul(p6, ind[:, 0:2],
                                 scr[:, j * 512:(j + 1) * 512],
                                 start=True, stop=True)
            return nc.tensor.matmul(p7, ind[:, 0:2], vf[:, 0:512],
                                    start=True, stop=True)
        prog.step("pe", mm, reads=["scrA", "v_f32", "ind2"],
                  writes=["ps6", "ps7"], tag=tag)

        def rows(h, j=j):
            nc.vector.tensor_scalar_mul(e2v, p7, 1.0 / 64.0)
            if want_mu:
                nc.vector.tensor_scalar_mul(muv, p6, 1.0 / 64.0)
                nc.vector.tensor_mul(msq, muv, muv)
                return nc.vector.tensor_sub(e2v, e2v, msq)
            return nc.vector.engine_nop()
        prog.step("dve", rows, reads=["ps6", "ps7"],
                  writes=["v_f32", "ps6", "ps7"], tag=tag)
        sc = 1.0 if sqrt_scale is None else sqrt_scale
        prog.step("act",
                  lambda h, sc=sc: nc.scalar.activation(
                      out=e2v, in_=e2v, func=AF.Sqrt,
                      bias=eps_col, scale=sc),
                  reads=["v_f32", "consts"], writes=["v_f32"], tag=tag)
        prog.step("dve",
                  lambda h: nc.vector.reciprocal(e2v, e2v),
                  reads=["v_f32"], writes=["v_f32"], tag=tag)
        for hh in range(2):
            prog.dma(bcd[2 + hh:3 + hh, j * 512:(j + 1) * 512],
                     e2v[hh:hh + 1, :], reads=["v_f32"],
                     writes=[f"bc{2 + hh}"], tag=tag)
            if want_mu:
                prog.dma(bcd[hh:hh + 1, j * 512:(j + 1) * 512],
                         muv[hh:hh + 1, :], reads=["v_f32"],
                         writes=[f"bc{hh}"], tag=tag)

    rmap = ctx.mha_rowmap
    for (lo, hi, hh) in rmap:
        prog.dma(ctx.sb["qk_rstdb"][lo:hi, :],
                 bcd[2 + hh:3 + hh, :].broadcast_to([hi - lo, TOK]),
                 reads=[f"bc{2 + hh}"], writes=["qk_rstdb"], tag=tag)
        if want_mu:
            prog.dma(ctx.sb["qk_mub"][lo:hi, :],
                     bcd[hh:hh + 1, :].broadcast_to([hi - lo, TOK]),
                     reads=[f"bc{hh}"], writes=["qk_mub"], tag=tag)


def make_norm_l0(gq_col, bq_col, gk_col, bk_col):
    def norm_qk(ctx, wkey, pair, dst_bf, tag):
        nc, prog = ctx.nc, ctx.prog
        scr = ctx.sb["scrA"]
        g_col = gq_col if wkey == "q" else gk_col
        b_col = bq_col if wkey == "q" else bk_col
        _qk_stats(ctx, pair, tag + wkey, want_mu=True)

        def apply(h):
            nc.vector.tensor_sub(ctx.sb["v_f32"][:], scr[:],
                                 ctx.sb["qk_mub"][:])
            nc.vector.scalar_tensor_tensor(
                out=ctx.sb["v_f32"][:], in0=ctx.sb["v_f32"][:],
                scalar=ctx.sb[g_col][:, 0:1], in1=ctx.sb["qk_rstdb"][:],
                op0=ALU.mult, op1=ALU.mult)
            return nc.vector.tensor_scalar(
                out=dst_bf[:], in0=ctx.sb["v_f32"][:],
                scalar1=ctx.sb[b_col][:, 0:1], scalar2=None, op0=ALU.add)
        prog.step("dve", apply,
                  reads=["scrA", "qk_mub", "qk_rstdb", g_col, b_col,
                         "v_f32"],
                  writes=["q_bf" if wkey == "q" else "k_bf", "v_f32"],
                  tag=tag)
    return norm_qk


def make_norm_l2(w_col_q, w_col_k):
    def norm_qk(ctx, wkey, pair, dst_bf, tag):
        nc, prog = ctx.nc, ctx.prog
        scr = ctx.sb["scrA"]
        wc = w_col_q if wkey == "q" else w_col_k
        _qk_stats(ctx, pair, tag + wkey, want_mu=False, sqrt_scale=8.0)
        dname = "q_bf" if wkey == "q" else "k_bf"

        def apply(h):
            return nc.vector.scalar_tensor_tensor(
                out=ctx.sb["v_f32"][:], in0=scr[:],
                scalar=ctx.sb[wc][:, 0:1], in1=ctx.sb["qk_rstdb"][:],
                op0=ALU.mult, op1=ALU.mult)
        prog.step("dve", apply, reads=["scrA", "qk_rstdb", wc, "v_f32"],
                  writes=["v_f32"], tag=tag)

        for (dl, sl_) in ((0, 32), (32, 0), (64, 96), (96, 64)):
            prog.dma(ctx.sb["ropesw"][dl:dl + 32, 0:TOK],
                     ctx.sb["v_f32"][sl_:sl_ + 32, :],
                     reads=["v_f32"], writes=["ropesw"], tag=tag)

        def rope(h):
            vfh = ctx.sb["v_f32"]
            nc.vector.tensor_mul(scr[:], vfh[:], ctx.sb["rope_cs2"][:])
            nc.vector.tensor_mul(ctx.sb["ropesw"][:], ctx.sb["ropesw"][:],
                                 ctx.sb["rope_sn2"][:])
            return nc.vector.tensor_add(dst_bf[:], scr[:],
                                        ctx.sb["ropesw"][:])
        prog.step("dve", rope,
                  reads=["v_f32", "rope_cs2", "rope_sn2", "ropesw", "scrA"],
                  writes=[dname, "scrA", "ropesw"], tag=tag)
    return norm_qk


def alloc_mha(ctx, need_rope=False):
    ctx.sbuf("v_f32", [128, TOK], F32)
    ctx.sbuf("q_bf", [128, TOK], BF16)
    ctx.sbuf("k_bf", [128, TOK], BF16)
    ctx.sbuf("VT0", [128, 1024], BF16)
    ctx.sbuf("VT1", [128, 1024], BF16)
    ctx.sbuf("attT4", [128, 512], BF16)
    ctx.sbuf("nm_col", [128, 4], F32)
    ctx.sbuf("mask4", [128, 512], BF16)
    ctx.sbuf("ind2", [128, 2], F32)
    for i in range(8):
        ctx.sbuf(f"aout{i}", [128, TOK], BF16)
    ctx.sb_w["mha_ws"] = [ctx.sbuf(f"mha_ws_{k}", [128, 128], BF16)
                          for k in range(NDT)]
    for par in range(2):
        ctx.sb_w[f"mha_wo_{par}"] = [
            ctx.sbuf(f"mha_wo_{par}_0", [128, 1024], BF16)] if par == 0 \
            else ctx.sb_w["mha_wo_0"]
    if need_rope:
        ctx.sbuf("rope_cs2", [128, TOK], F32)
        ctx.sbuf("rope_sn2", [128, TOK], F32)
        ctx.sbuf("ropesw", [128, TOK], F32)
    ctx.sb["castb"] = ctx.sb["attT4"]
    ctx.sb["castf"] = ctx.sb["scrA"]


def make_mask4_host():
    import ml_dtypes
    m = np.zeros((128, 512), dtype=np.float32)
    tri = np.where(np.arange(128)[:, None] >= np.arange(128)[None, :],
                   0.0, -1e30).astype(np.float32)
    m[:, 384:512] = tri
    return m.astype(ml_dtypes.bfloat16)


def build_l0_attn_program(debug=False):
    nc = bass.Bass("TRN2", target_bir_lowering=False)
    xin, consts_in, ident_in, xout = _mk_base(nc)
    w_in = {}
    for nm in ("l0_wqT", "l0_wkT", "l0_wvT", "l0_woT"):
        w_in[nm] = nc.dram_tensor(nm, [D, D], F32, kind="ExternalInput").ap()
    for nm in ("l0_gq", "l0_bq", "l0_gk", "l0_bk"):
        w_in[nm] = nc.dram_tensor(nm + "_in", [128, 1], F32,
                                  kind="ExternalInput").ap()
    w_in["l0_bo"] = nc.dram_tensor("l0_bo_in", [128, NDT], F32,
                                   kind="ExternalInput").ap()
    w_in["mask4"] = nc.dram_tensor("mask4_in", [128, 512], F32,
                                   kind="ExternalInput").ap()
    w_in["ind2"] = nc.dram_tensor("ind2_in", [128, 2], F32,
                                  kind="ExternalInput").ap()
    with contextlib.ExitStack() as stack:
        prog = Prog(nc)
        ctx = Ctx(nc, prog, stack)
        alloc_common(ctx, bf_groups=1)
        alloc_mha(ctx)
        ctx.sbuf("qk_mub", [128, TOK], F32)
        ctx.sbuf("qk_rstdb", [128, TOK], F32)
        for nm in ("l0_gq", "l0_bq", "l0_gk", "l0_bk"):
            ctx.sbuf(nm, [128, 1], F32)
            prog.dma(ctx.sb[nm][:], w_in[nm], writes=[nm])
        ctx.sbuf("l0_bo", [128, NDT], F32)
        prog.dma(ctx.sb["l0_bo"][:], w_in["l0_bo"], writes=["l0_bo"])
        prog.dma(ctx.sb["mask4"][:], w_in["mask4"], writes=["mask4"])
        prog.dma(ctx.sb["ind2"][:], w_in["ind2"], writes=["ind2"])
        _setup_common(ctx, xin, consts_in, ident_in, want_ident=True)
        ctx.mha_rowmap = [(0, 64, 0), (64, 128, 1)]

        wdr = {}
        for short, nm in (("wq", "l0_wqT"), ("wk", "l0_wkT"),
                          ("wv", "l0_wvT"), ("wo", "l0_woT")):
            dr = nc.dram_tensor(f"{nm}_bf", [D, D], BF16, kind="Internal").ap()
            precast_to_dram(ctx, f"{nm}bf", w_in[nm], dr, D, D)
            wdr[f"{short}_bf"] = dr
            wdr[f"{short}_src"] = f"{nm}bf"

        cfg = dict(sm_scale=1.0,
                   norm_qk=make_norm_l0("l0_gq", "l0_bq", "l0_gk", "l0_bk"),
                   bo_col=ctx.sb["l0_bo"], **wdr)
        xT = [ctx.sb[f"xT{i}"] for i in range(NDT)]
        emit_mha(ctx, _bufnames("xT", NDT), xT, "l0a_", cfg)
        store_stream_outputs(ctx, xout)
        if debug:
            for nm, bufn in (("dbg_qbf", "q_bf"), ("dbg_kbf", "k_bf"),
                             ("dbg_vf", "v_f32")):
                dr = nc.dram_tensor(nm, [128, TOK], F32,
                                    kind="ExternalOutput").ap()
                prog.step("dve", lambda h, b=bufn: nc.vector.tensor_copy(
                    out=ctx.sb["scrA"][:], in_=ctx.sb[b][:]),
                    reads=[bufn], writes=["scrA"])
                prog.dma(dr, ctx.sb["scrA"][:], reads=["scrA"],
                         writes=["dbgx" + nm])
            for i in range(8):
                dr = nc.dram_tensor(f"dbg_aout{i}", [128, TOK], F32,
                                    kind="ExternalOutput").ap()
                prog.step("dve", lambda h, i=i: nc.vector.tensor_copy(
                    out=ctx.sb["scrA"][:], in_=ctx.sb[f"aout{i}"][:]),
                    reads=[f"aout{i}"], writes=["scrA"])
                prog.dma(dr, ctx.sb["scrA"][:], reads=["scrA"],
                         writes=[f"dbgy{i}"])
        prog.finalize(stack)
    return nc


def stage_l0_attn_weights(inp):
    gq = np.asarray(inp["L0_attn_gq"], np.float32)
    bq = np.asarray(inp["L0_attn_bq"], np.float32)
    gk = np.asarray(inp["L0_attn_gk"], np.float32)
    bk = np.asarray(inp["L0_attn_bk"], np.float32)
    bo = np.asarray(inp["L0_attn_bo"], np.float32)
    return {
        "l0_wqT": np.ascontiguousarray(np.asarray(inp["L0_attn_Wq"], np.float32).T),
        "l0_wkT": np.ascontiguousarray(np.asarray(inp["L0_attn_Wk"], np.float32).T),
        "l0_wvT": np.ascontiguousarray(np.asarray(inp["L0_attn_Wv"], np.float32).T),
        "l0_woT": np.ascontiguousarray(np.asarray(inp["L0_attn_Wo"], np.float32).T),
        "l0_gq": np.ascontiguousarray(np.tile(gq, 2)[:, None]),
        "l0_bq": np.ascontiguousarray(np.tile(bq, 2)[:, None]),
        "l0_gk": np.ascontiguousarray(np.tile(gk, 2)[:, None]),
        "l0_bk": np.ascontiguousarray(np.tile(bk, 2)[:, None]),
        "l0_bo": np.ascontiguousarray(bo.reshape(NDT, 128).T),
        "mask4": make_mask4_host(),
        "ind2": np.ascontiguousarray(np.concatenate(
            [np.repeat([[1.0, 0.0]], 64, 0), np.repeat([[0.0, 1.0]], 64, 0)],
            0).astype(np.float32)),
    }


def run_l0_attn(x_by_core, weights_np, debug=False):
    nc = build_l0_attn_program(debug=debug)
    consts = make_const_host()
    ident = make_ident_host()
    in_maps = []
    for c in range(8):
        m = dict(xT=x_by_core[c], const_in=consts, ident_in=ident)
        ren = {"mask4": "mask4_in", "ind2": "ind2_in", "l0_gq": "l0_gq_in",
               "l0_bq": "l0_bq_in", "l0_gk": "l0_gk_in", "l0_bk": "l0_bk_in",
               "l0_bo": "l0_bo_in"}
        for k, v in weights_np.items():
            m[ren.get(k, k)] = v
        in_maps.append(m)
    res = run_bass_kernel_spmd(nc, in_maps, core_ids=list(range(8)))
    return res.results




def build_l2_attn_program(debug=False):
    nc = bass.Bass("TRN2", target_bir_lowering=False)
    xin, consts_in, ident_in, xout = _mk_base(nc)
    w_in = {
        "l2_wqT": nc.dram_tensor("l2_wqT", [D, D], F32,
                                 kind="ExternalInput").ap(),
        "l2_wkT": nc.dram_tensor("l2_wkT", [D, D], F32,
                                 kind="ExternalInput").ap(),
        "l2_wvT": nc.dram_tensor("l2_wvT", [D, D], F32,
                                 kind="ExternalInput").ap(),
        "l2_woT": nc.dram_tensor("l2_woT", [D, D], F32,
                                 kind="ExternalInput").ap(),
        "l2_qw": nc.dram_tensor("l2_qw_in", [128, 1], F32,
                                kind="ExternalInput").ap(),
        "l2_kw": nc.dram_tensor("l2_kw_in", [128, 1], F32,
                                kind="ExternalInput").ap(),
        "rope_cos": nc.dram_tensor("rope_cos_in", [128, TOK], F32,
                                   kind="ExternalInput").ap(),
        "rope_sin": nc.dram_tensor("rope_sin_in", [128, TOK], F32,
                                   kind="ExternalInput").ap(),
        "mask4": nc.dram_tensor("mask4_in", [128, 512], BF16,
                                kind="ExternalInput").ap(),
        "ind2": nc.dram_tensor("ind2_in", [128, 2], F32,
                               kind="ExternalInput").ap(),
    }
    with contextlib.ExitStack() as stack:
        prog = Prog(nc)
        ctx = Ctx(nc, prog, stack)
        alloc_common(ctx, bf_groups=1)
        alloc_mha(ctx, need_rope=True)
        ctx.sbuf("qk_rstdb", [128, TOK], F32)
        ctx.sb["qk_mub"] = ctx.sb["qk_rstdb"]
        for nm in ("l2_qw", "l2_kw"):
            ctx.sbuf(nm, [128, 1], F32)
            prog.dma(ctx.sb[nm][:], w_in[nm], writes=[nm])
        prog.dma(ctx.sb["rope_cs2"][:], w_in["rope_cos"],
                 writes=["rope_cs2"])
        prog.dma(ctx.sb["rope_sn2"][:], w_in["rope_sin"],
                 writes=["rope_sn2"])
        prog.dma(ctx.sb["mask4"][:], w_in["mask4"], writes=["mask4"])
        prog.dma(ctx.sb["ind2"][:], w_in["ind2"], writes=["ind2"])
        _setup_common(ctx, xin, consts_in, ident_in, want_ident=True)
        ctx.mha_rowmap = [(0, 64, 0), (64, 128, 1)]

        wdr = {}
        for short, nm in (("wq", "l2_wqT"), ("wk", "l2_wkT"),
                          ("wv", "l2_wvT"), ("wo", "l2_woT")):
            dr = nc.dram_tensor(f"{nm}_bf", [D, D], BF16, kind="Internal").ap()
            precast_to_dram(ctx, f"{nm}bf", w_in[nm], dr, D, D)
            wdr[f"{short}_bf"] = dr
            wdr[f"{short}_src"] = f"{nm}bf"

        cfg = dict(sm_scale=0.125,
                   norm_qk=make_norm_l2("l2_qw", "l2_kw"),
                   bo_col=None, **wdr)
        xT = [ctx.sb[f"xT{i}"] for i in range(NDT)]
        emit_mha(ctx, _bufnames("xT", NDT), xT, "l2a_", cfg)
        store_stream_outputs(ctx, xout)
        if debug:
            for nm, bufn in (("dbg_qbf", "q_bf"), ("dbg_kbf", "k_bf"),
                             ("dbg_vf", "v_f32"), ("dbg_aout0", "aout0")):
                dr = nc.dram_tensor(nm, [128, TOK], F32,
                                    kind="ExternalOutput").ap()
                prog.step("dve", lambda h, b=bufn: nc.vector.tensor_copy(
                    out=ctx.sb["scrA"][:], in_=ctx.sb[b][:]),
                    reads=[bufn], writes=["scrA"])
                prog.dma(dr, ctx.sb["scrA"][:], reads=["scrA"],
                         writes=["dbgx" + nm])
        prog.finalize(stack)
    return nc


def _l2_qk_perm():
    perm = np.zeros(128, dtype=np.int64)
    for p in range(128):
        if p < 32:
            h, d_ = 0, p
        elif p < 64:
            h, d_ = 1, p - 32
        elif p < 96:
            h, d_ = 0, 32 + (p - 64)
        else:
            h, d_ = 1, 32 + (p - 96)
        perm[p] = 64 * h + d_
    return perm


def stage_l2_attn_weights(inp):
    wqkv = np.asarray(inp["L2_attn_wqkv"], np.float32)
    wo = np.asarray(inp["L2_attn_wo"], np.float32)
    qw = np.asarray(inp["L2_attn_q_norm_w"], np.float32)
    kw = np.asarray(inp["L2_attn_k_norm_w"], np.float32)
    wqT = np.ascontiguousarray(wqkv[0:1024].T)
    wkT = np.ascontiguousarray(wqkv[1024:2048].T)
    wvT = np.ascontiguousarray(wqkv[2048:3072].T)
    wcol = np.tile(qw, 2)
    kcol = np.tile(kw, 2)
    T, Dh = TOK, 64
    inv_freq = 1.0 / (10000.0 ** (np.arange(0, Dh, 2) / Dh))
    freqs = np.arange(T)[:, None] * inv_freq
    cosT = np.cos(freqs).T
    sinT = np.sin(freqs).T
    cs_head = np.concatenate([cosT, cosT])
    sn_head = np.concatenate([-sinT, sinT])
    rope_cos = np.ascontiguousarray(
        np.concatenate([cs_head, cs_head]).astype(np.float32))
    rope_sin = np.ascontiguousarray(
        np.concatenate([sn_head, sn_head]).astype(np.float32))

    ind2 = np.zeros((128, 2), np.float32)
    ind2[0:64, 0] = 1
    ind2[64:128, 1] = 1
    return {
        "l2_wqT": wqT, "l2_wkT": wkT, "l2_wvT": wvT,
        "l2_woT": np.ascontiguousarray(wo.T),
        "l2_qw": np.ascontiguousarray(wcol[:, None]),
        "l2_kw": np.ascontiguousarray(kcol[:, None]),
        "rope_cos": rope_cos, "rope_sin": rope_sin,
        "mask4": make_mask4_host(),
        "ind2": ind2,
    }


def run_l2_attn(x_by_core, weights_np, debug=False):
    nc = build_l2_attn_program(debug=debug)
    consts = make_const_host()
    ident = make_ident_host()
    ren = {"mask4": "mask4_in", "ind2": "ind2_in",
           "rope_cos": "rope_cos_in", "rope_sin": "rope_sin_in",
           "l2_qw": "l2_qw_in", "l2_kw": "l2_kw_in"}
    in_maps = []
    for c in range(8):
        m = dict(xT=x_by_core[c], const_in=consts, ident_in=ident)
        for k, v in weights_np.items():
            m[ren.get(k, k)] = v
        in_maps.append(m)
    res = run_bass_kernel_spmd(nc, in_maps, core_ids=list(range(8)))
    return res.results





def emit_l1_ln(ctx, x_bufs, xT, out_bf, out_bufs, tag):
    nc, prog = ctx.nc, ctx.prog
    ones = ctx.sb["ones_f32"]
    scr = ctx.sb["scrA"]
    rowA, rowB = ctx.sb["rowA"], ctx.sb["rowB"]
    rowC = ctx.sb["scrA"][0:1, 1536:2048]
    bcd = ctx.dram["bc_scr"]
    p1 = ctx.psb(6, 1, part=1)
    p2 = ctx.psb(7, 1, part=1)
    eps = 1e-5

    for j in range(4):
        for i in range(NDT):
            prog.step("act",
                      lambda h, i=i, j=j: nc.scalar.activation(
                          out=scr[:, 0:512],
                          in_=xT[i][:, j * 512:(j + 1) * 512], func=AF.Square),
                      reads=[x_bufs[i]], writes=["scrA"], tag=tag)

            def mm_s(h, i=i, j=j):
                nc.tensor.matmul(
                    p1, ones[:, 0:1], xT[i][:, j * 512:(j + 1) * 512],
                    start=(i == 0), stop=(i == NDT - 1))
                return nc.tensor.matmul(
                    p2, ones[:, 0:1], scr[:, 0:512],
                    start=(i == 0), stop=(i == NDT - 1))
            prog.step("pe", mm_s, reads=list(x_bufs) + ["ones_f32", "scrA"],
                      writes=["ps6", "ps7", "scrA"], tag=tag + "s")

        def row_ops(h, j=j):
            nc.vector.tensor_scalar_mul(rowA, p1, 1.0 / D)
            nc.vector.tensor_scalar_mul(rowB, p2, 1.0 / D)
            nc.vector.tensor_mul(p2[0:1, 0:512], rowA, rowA)
            return nc.vector.tensor_sub(rowB, rowB, p2[0:1, 0:512])
        prog.step("dve", row_ops, reads=["ps6", "ps7"],
                  writes=["scrA", "ps6", "ps7"], tag=tag)
        prog.step("act",
                  lambda h: nc.scalar.activation(
                      out=rowC, in_=rowB, func=AF.Sqrt,
                      bias=ccol(ctx, "eps_sq")[0:1, :], scale=1.0 + eps),
                  reads=["scrA", "consts"], writes=["scrA"], tag=tag)
        prog.step("act",
                  lambda h: nc.scalar.activation(
                      out=rowB, in_=rowB, func=AF.Sqrt,
                      bias=ccol(ctx, "eps_ln")[0:1, :]),
                  reads=["scrA", "consts"], writes=["scrA"], tag=tag)

        def recips(h):
            nc.vector.reciprocal(rowB, rowB)
            return nc.vector.reciprocal(rowC, rowC)
        prog.step("dve", recips, reads=["scrA"], writes=["scrA"], tag=tag)
        prog.dma(bcd[0:1, j * 512:(j + 1) * 512], rowA,
                 reads=["scrA"], writes=["bc0"], tag=tag)
        prog.dma(bcd[1:2, j * 512:(j + 1) * 512], rowB,
                 reads=["scrA"], writes=["bc1"], tag=tag)
        prog.dma(bcd[2:3, j * 512:(j + 1) * 512], rowC,
                 reads=["scrA"], writes=["bc2"], tag=tag)

    r_mub, r_rstdb = ctx.sb["r_mub"], ctx.sb["r_rstdb"]
    r_rstdc = ctx.sb["r_rstdc"]
    gcol = ctx.sb["l1_gamma"]
    bcol = ctx.sb["l1_beta"]
    t2 = ctx.sb["l1_t2"]
    for j in range(4):
        prog.dma(r_mub[:, 0:512],
                 bcd[0:1, j * 512:(j + 1) * 512].broadcast_to([128, 512]),
                 reads=["bc0"], writes=["r_mub"], tag=tag)
        prog.dma(r_rstdb[:, 0:512],
                 bcd[1:2, j * 512:(j + 1) * 512].broadcast_to([128, 512]),
                 reads=["bc1"], writes=["r_rstdb"], tag=tag)
        prog.dma(r_rstdc[:, 0:512],
                 bcd[2:3, j * 512:(j + 1) * 512].broadcast_to([128, 512]),
                 reads=["bc2"], writes=["r_rstdc"], tag=tag)
        for i in range(NDT):
            def apply(h, i=i, j=j):
                sl_ = slice(j * 512, (j + 1) * 512)
                nc.vector.tensor_sub(scr[:, 0:512], xT[i][:, sl_],
                                     r_mub[:, 0:512])
                nc.vector.tensor_mul(t2[:, 0:512], scr[:, 0:512],
                                     r_rstdc[:, 0:512])
                nc.vector.tensor_scalar(
                    out=out_bf[i][:, sl_], in0=t2[:, 0:512],
                    scalar1=gcol[:, i:i + 1], scalar2=bcol[:, i:i + 1],
                    op0=ALU.mult, op1=ALU.add)
                return nc.vector.tensor_mul(t2[:, 0:512], scr[:, 0:512],
                                            r_rstdb[:, 0:512])
            prog.step("dve", apply,
                      reads=[x_bufs[i], "r_mub", "r_rstdb", "r_rstdc",
                             "l1_gamma", "l1_beta"],
                      writes=[out_bufs[i], "scrA", "l1_t2"], tag=tag)
            prog.step("dve",
                      lambda h, i=i, j=j: nc.vector.tensor_add(
                          xT[i][:, j * 512:(j + 1) * 512],
                          xT[i][:, j * 512:(j + 1) * 512], t2[:, 0:512]),
                      reads=["l1_t2", x_bufs[i]], writes=[x_bufs[i]],
                      tag=tag)


def emit_l1_attn(ctx, x_bufs, xT, wdr, tag):
    nc, prog = ctx.nc, ctx.prog
    ln_bf = [ctx.sb[f"bfA{i}"] for i in range(NDT)]
    ln_bufs = _bufnames("bfA", NDT)
    emit_l1_ln(ctx, x_bufs, xT, ln_bf, ln_bufs, tag)

    scr = ctx.sb["scrA"]
    PS = ctx.sb["PS"]
    uf = ctx.sb["l1_u"]
    uc = ctx.sb["l1_uc"]
    ybf_d = ctx.dram["l1_ybf"]

    prog.step("act", lambda h: nc.scalar.activation(
        out=ctx.sb["l1_a"][:], in_=ctx.sb["l1_Alog"][:], func=AF.Exp),
        reads=["l1_Alog"], writes=["l1_a"], tag=tag)
    prog.step("act", lambda h: nc.scalar.activation(
        out=ctx.sb["l1_a"][:], in_=ctx.sb["l1_a"][:], func=AF.Exp,
        scale=-1.0),
        reads=["l1_a"], writes=["l1_a"], tag=tag)
    prog.step("dve", lambda h: nc.vector.tensor_mul(
        ctx.sb["l1_gsc"][:], ctx.sb["l1_Bp"][:], ctx.sb["l1_Cp"][:]),
        reads=["l1_Bp", "l1_Cp"], writes=["l1_gsc"], tag=tag)

    n_dve = 40

    for i in range(16):
        wt = ctx.sb_w["l1_inw"]
        for k in range(NDT):
            prog.dma(wt[k][:], wdr["inw_bf"][i][k * 128:(k + 1) * 128, :],
                     reads=["l1inwbf"], writes=["l1_inw"], queue="sp",
                     tag=tag)

        def mm_u(h, i=i):
            last = None
            for j in range(4):
                for k in range(NDT):
                    last = nc.tensor.matmul(
                        PS[:, j * 512:(j + 1) * 512], wt[k][:],
                        ln_bf[k][:, j * 512:(j + 1) * 512],
                        start=(k == 0), stop=(k == NDT - 1))
            return last
        prog.step("pe", mm_u, reads=list(ln_bufs) + ["l1_inw"],
                  writes=["ps0", "ps1", "ps2", "ps3"], tag=tag + f"u{i}")
        prog.step("dve",
                  lambda h, i=i: nc.vector.tensor_scalar(
                      out=uf[:], in0=PS[:, 0:TOK],
                      scalar1=ctx.sb["l1_inb"][:, i:i + 1], scalar2=None,
                      op0=ALU.add),
                  reads=["ps0", "ps1", "ps2", "ps3", "l1_inb"],
                  writes=["l1_u", "ps0", "ps1", "ps2", "ps3"], tag=tag)

        def conv(h, i=i):
            cw = ctx.sb["l1_convw"]
            nc.vector.tensor_scalar(
                out=scr[:], in0=uf[:], scalar1=cw[:, i * 3 + 1:i * 3 + 2],
                scalar2=ctx.sb["l1_convb"][:, i:i + 1],
                op0=ALU.mult, op1=ALU.add)
            nc.vector.scalar_tensor_tensor(
                out=scr[:, 1:TOK], in0=uf[:, 0:TOK - 1],
                scalar=cw[:, i * 3:i * 3 + 1], in1=scr[:, 1:TOK],
                op0=ALU.mult, op1=ALU.add)
            return nc.vector.scalar_tensor_tensor(
                out=scr[:, 0:TOK - 1], in0=uf[:, 1:TOK],
                scalar=cw[:, i * 3 + 2:i * 3 + 3], in1=scr[:, 0:TOK - 1],
                op0=ALU.mult, op1=ALU.add)
        prog.step("dve", conv, reads=["l1_u", "l1_convw", "l1_convb", "scrA"],
                  writes=["scrA"], tag=tag)
        prog.step("act",
                  lambda h: nc.scalar.activation(
                      out=uc[:], in_=scr[:], func=AF.Silu),
                  reads=["scrA"], writes=["l1_uc"], tag=tag)

        first_d = True
        first_p = True
        for n in range(64):
            acol = ctx.sb["l1_a"][:, i * 64 + n:i * 64 + n + 1]
            gcoln = ctx.sb["l1_gsc"][:, i * 64 + n:i * 64 + n + 1]
            hbuf = "l1_hd" if n % 2 == 0 else "l1_hp"
            prog.step("dve",
                      lambda h, acol=acol, hbuf=hbuf:
                      nc.vector.tensor_tensor_scan(
                          out=ctx.sb[hbuf][:],
                          data0=acol.broadcast_to([128, TOK]),
                          data1=uc[:], initial=0.0,
                          op0=ALU.mult, op1=ALU.add),
                      reads=["l1_uc", "l1_a"], writes=[hbuf],
                      tag=tag + f"sc{i}.{n}")
            on_dve = (n % 8) < 5
            eng = "dve" if on_dve else "pool"
            ee = nc.vector
            ybuf = "l1_yd" if on_dve else "l1_yp"
            first = first_d if on_dve else first_p
            if on_dve:
                if first:
                    prog.step("dve",
                              lambda h, gcoln=gcoln, hbuf=hbuf:
                              nc.vector.tensor_scalar(
                                  out=ctx.sb["l1_yd"][:],
                                  in0=ctx.sb[hbuf][:],
                                  scalar1=gcoln, scalar2=None, op0=ALU.mult),
                              reads=[hbuf, "l1_gsc"], writes=["l1_yd"],
                              tag=tag)
                else:
                    prog.step("dve",
                              lambda h, gcoln=gcoln, hbuf=hbuf:
                              nc.vector.scalar_tensor_tensor(
                                  out=ctx.sb["l1_yd"][:],
                                  in0=ctx.sb[hbuf][:],
                                  scalar=gcoln, in1=ctx.sb["l1_yd"][:],
                                  op0=ALU.mult, op1=ALU.add),
                              reads=[hbuf, "l1_gsc", "l1_yd"],
                              writes=["l1_yd"], tag=tag)
                first_d = False
            else:
                if first:
                    prog.step("pool",
                              lambda h, gcoln=gcoln, hbuf=hbuf:
                              nc.gpsimd.tensor_tensor(
                                  out=ctx.sb["l1_yp"][:],
                                  in0=ctx.sb[hbuf][:],
                                  in1=gcoln.broadcast_to([128, TOK]),
                                  op=ALU.mult),
                              reads=[hbuf, "l1_gsc"], writes=["l1_yp"],
                              tag=tag)
                else:
                    def pmac(h, gcoln=gcoln, hbuf=hbuf):
                        nc.gpsimd.tensor_tensor(
                            out=ctx.sb["l1_ptmp"][:], in0=ctx.sb[hbuf][:],
                            in1=gcoln.broadcast_to([128, TOK]), op=ALU.mult)
                        return nc.gpsimd.tensor_add(
                            ctx.sb["l1_yp"][:], ctx.sb["l1_yp"][:],
                            ctx.sb["l1_ptmp"][:])
                    prog.step("pool", pmac,
                              reads=[hbuf, "l1_gsc", "l1_yp", "l1_ptmp"],
                              writes=["l1_yp", "l1_ptmp"], tag=tag)
                first_p = False

        for which, ti in (("v", i + 16), ("g", i + 32)):
            for k in range(NDT):
                prog.dma(wt[k][:], wdr["inw_bf"][ti][k * 128:(k + 1) * 128, :],
                         reads=["l1inwbf"], writes=["l1_inw"], queue="sp",
                         tag=tag)

            def mm_vg(h, ti=ti):
                last = None
                for j in range(4):
                    for k in range(NDT):
                        last = nc.tensor.matmul(
                            PS[:, 2048 + j * 512:2048 + (j + 1) * 512],
                            wt[k][:], ln_bf[k][:, j * 512:(j + 1) * 512],
                            start=(k == 0), stop=(k == NDT - 1))
                return last
            prog.step("pe", mm_vg, reads=list(ln_bufs) + ["l1_inw"],
                      writes=["ps4", "ps5", "ps6", "ps7"], tag=tag + f"vg{ti}")
            dst = ctx.sb["l1_vs"] if which == "v" else ctx.sb["l1_gs"]
            dname = "l1_vs" if which == "v" else "l1_gs"
            prog.step("dve",
                      lambda h, ti=ti, dst=dst: nc.vector.tensor_scalar(
                          out=dst[:], in0=PS[:, 2048:4096],
                          scalar1=ctx.sb["l1_inb"][:, ti:ti + 1], scalar2=None,
                          op0=ALU.add),
                      reads=["ps4", "ps5", "ps6", "ps7", "l1_inb"],
                      writes=[dname, "ps4", "ps5", "ps6", "ps7"], tag=tag)
            prog.step("act",
                      lambda h, dst=dst, which=which: nc.scalar.activation(
                          out=dst[:], in_=dst[:],
                          func=AF.Silu if which == "v" else AF.Sigmoid),
                      reads=[dname], writes=[dname], tag=tag)

        def mix(h, i=i):
            nc.vector.tensor_add(ctx.sb["l1_yd"][:], ctx.sb["l1_yd"][:],
                                 ctx.sb["l1_yp"][:])
            nc.vector.scalar_tensor_tensor(
                out=ctx.sb["l1_yd"][:], in0=uc[:],
                scalar=ctx.sb["l1_Dp"][:, i:i + 1], in1=ctx.sb["l1_yd"][:],
                op0=ALU.mult, op1=ALU.add)
            nc.vector.tensor_mul(ctx.sb["l1_yd"][:], ctx.sb["l1_yd"][:],
                                 ctx.sb["l1_gs"][:])
            return nc.vector.tensor_tensor(
                out=ctx.sb["l1_ybf"][:], in0=ctx.sb["l1_yd"][:],
                in1=ctx.sb["l1_vs"][:], op=ALU.add)
        prog.step("dve", mix,
                  reads=["l1_yd", "l1_yp", "l1_gs", "l1_vs", "l1_uc",
                         "l1_Dp"],
                  writes=["l1_yd", "l1_ybf"], tag=tag)
        prog.dma(ybf_d[i], ctx.sb["l1_ybf"][:], reads=["l1_ybf"],
                 writes=["l1ybfd"], queue="sp", tag=tag)

    for c in range(4):
        for k in range(16):
            par2 = 0
            prog.dma(ctx.sb_w[f"l1_outw_{par2}"][0][:],
                     wdr["outw_bf"][k * 128:(k + 1) * 128, :],
                     reads=["l1outwbf"], writes=[f"l1_outw_{par2}"],
                     queue="sp", tag=tag)
            prog.dma(ctx.sb["l1_ybf"][:, 0:512],
                     ybf_d[k][0:128, c * 512:(c + 1) * 512],
                     reads=["l1ybfd"], writes=["l1_ybf"], queue="actq",
                     tag=tag)

            def mm_o(h, k=k, c=c, par2=par2):
                w2t = ctx.sb_w[f"l1_outw_{par2}"][0]
                last = None
                for o in range(NDT):
                    last = nc.tensor.matmul(
                        ctx.psb(o, 1),
                        w2t[:, o * 128:(o + 1) * 128],
                        ctx.sb["l1_ybf"][:, 0:512],
                        start=(k == 0), stop=(k == 15))
                return last
            prog.step("pe", mm_o, reads=["l1_ybf", f"l1_outw_{par2}"],
                      writes=[f"ps{o}" for o in range(NDT)],
                      tag=tag + f"o{c}.{k}")
        for o in range(NDT):
            def ev(h, o=o, c=c):
                nc.vector.tensor_scalar(
                    out=scr[:, 0:512], in0=ctx.psb(o, 1),
                    scalar1=ctx.sb["l1_outb"][:, o:o + 1], scalar2=None,
                    op0=ALU.add)
                return nc.vector.tensor_add(
                    xT[o][:, c * 512:(c + 1) * 512],
                    xT[o][:, c * 512:(c + 1) * 512], scr[:, 0:512])
            prog.step("dve", ev,
                      reads=[f"ps{o}", x_bufs[o], "l1_outb", "scrA"],
                      writes=[x_bufs[o], f"ps{o}", "scrA"], tag=tag + f"ev{c}")


def build_l1_attn_program():
    nc = bass.Bass("TRN2", target_bir_lowering=False)
    xin, consts_in, ident_in, xout = _mk_base(nc)
    w_in = {
        "l1_inw": nc.dram_tensor("l1_inw", [48, D, 128], F32,
                                 kind="ExternalInput").ap(),
        "l1_outw": nc.dram_tensor("l1_outw", [2048, D], F32,
                                  kind="ExternalInput").ap(),
        "l1_Alog": nc.dram_tensor("l1_Alog_in", [128, 1024], F32,
                                  kind="ExternalInput").ap(),
        "l1_Bp": nc.dram_tensor("l1_Bp_in", [128, 1024], F32,
                                kind="ExternalInput").ap(),
        "l1_Cp": nc.dram_tensor("l1_Cp_in", [128, 1024], F32,
                                kind="ExternalInput").ap(),
        "l1_Dp": nc.dram_tensor("l1_Dp_in", [128, 16], F32,
                                kind="ExternalInput").ap(),
        "l1_convw": nc.dram_tensor("l1_convw_in", [128, 48], F32,
                                   kind="ExternalInput").ap(),
        "l1_convb": nc.dram_tensor("l1_convb_in", [128, 16], F32,
                                   kind="ExternalInput").ap(),
        "l1_inb": nc.dram_tensor("l1_inb_in", [128, 48], F32,
                                 kind="ExternalInput").ap(),
        "l1_outb": nc.dram_tensor("l1_outb_in", [128, 8], F32,
                                  kind="ExternalInput").ap(),
        "l1_gamma": nc.dram_tensor("l1_gamma_in", [128, 8], F32,
                                   kind="ExternalInput").ap(),
        "l1_beta": nc.dram_tensor("l1_beta_in", [128, 8], F32,
                                  kind="ExternalInput").ap(),
    }
    with contextlib.ExitStack() as stack:
        prog = Prog(nc)
        ctx = Ctx(nc, prog, stack)
        alloc_common(ctx, bf_groups=1)
        for nm, shape in (("r_rstdc", [128, 512]),
                          ("l1_u", [128, TOK]), ("l1_uc", [128, TOK]),
                          ("l1_hd", [128, TOK]), ("l1_hp", [128, TOK]),
                          ("l1_yd", [128, TOK]), ("l1_yp", [128, TOK]),
                          ("l1_vs", [128, TOK])):
            ctx.sbuf(nm, shape, F32)
        ctx.sb["l1_gs"] = ctx.sb["l1_u"]
        ctx.sb["l1_ptmp"] = ctx.sb["l1_vs"]
        ctx.sb["l1_t2"] = ctx.sb["l1_u"]
        ctx.sbuf("l1_ybf", [128, TOK], BF16)
        for nm, shape in (("l1_a", [128, 1024]),
                          ("l1_gsc", [128, 1024]), ("l1_Dp", [128, 16]),
                          ("l1_convw", [128, 48]), ("l1_convb", [128, 16]),
                          ("l1_inb", [128, 48]), ("l1_outb", [128, 8]),
                          ("l1_gamma", [128, 8]), ("l1_beta", [128, 8])):
            ctx.sbuf(nm, shape, F32)
        ctx.sb["l1_Alog"] = ctx.sb["l1_yd"][0:128, 0:1024]
        ctx.sb["l1_Bp"] = ctx.sb["l1_hd"][0:128, 0:1024]
        ctx.sb["l1_Cp"] = ctx.sb["l1_hp"][0:128, 0:1024]
        for nm in ("l1_Alog", "l1_Bp", "l1_Cp", "l1_Dp", "l1_convw",
                   "l1_convb", "l1_inb", "l1_outb", "l1_gamma", "l1_beta"):
            prog.dma(ctx.sb[nm][0:128, :] if nm in ("l1_Alog", "l1_Bp",
                                                    "l1_Cp")
                     else ctx.sb[nm][:], w_in[nm], writes=[nm])
        ctx.sb_w["l1_inw"] = [ctx.sbuf(f"l1_inw_{k}", [128, 128], BF16)
                              for k in range(NDT)]
        ctx.sb_w["l1_outw_0"] = [
            ctx.sbuf("l1_outw_0_0", [128, 1024], BF16)]
        ctx.sb_w["l1_outw_1"] = ctx.sb_w["l1_outw_0"]
        ctx.sb["castb"] = ctx.sb["l1_ybf"][0:128, 0:512]
        ctx.sb["castf"] = ctx.sb["scrA"]
        ctx.dram["l1_ybf"] = nc.dram_tensor(
            "l1_ybf_scr", [16, 128, TOK], BF16, kind="Internal").ap()
        _setup_common(ctx, xin, consts_in, ident_in)

        wdr = {}
        inwbf = nc.dram_tensor("l1_inw_bf", [48, D, 128], BF16,
                               kind="Internal").ap()
        for t in range(48):
            precast_to_dram(ctx, "l1inwbf", w_in["l1_inw"][t], inwbf[t],
                            D, 128)
        outwbf = nc.dram_tensor("l1_outw_bf", [2048, D], BF16,
                                kind="Internal").ap()
        precast_to_dram(ctx, "l1outwbf", w_in["l1_outw"], outwbf, 2048, D)
        wdr["inw_bf"] = inwbf
        wdr["outw_bf"] = outwbf

        xT = [ctx.sb[f"xT{i}"] for i in range(NDT)]
        emit_l1_attn(ctx, _bufnames("xT", NDT), xT, wdr, "l1a_")
        store_stream_outputs(ctx, xout)
        prog.finalize(stack)
    return nc


def stage_l1_attn_weights(inp):
    inw = np.asarray(inp["L1_attn_in_w"], np.float32)
    outw = np.asarray(inp["L1_attn_out_w"], np.float32)
    Alog = np.asarray(inp["L1_attn_A_log"], np.float32)
    Bp = np.asarray(inp["L1_attn_B_p"], np.float32)
    Cp = np.asarray(inp["L1_attn_C_p"], np.float32)
    Dp = np.asarray(inp["L1_attn_D_p"], np.float32)
    convw = np.asarray(inp["L1_attn_conv_w"], np.float32)[:, 0, :]
    convb = np.asarray(inp["L1_attn_conv_b"], np.float32)
    inb = np.asarray(inp["L1_attn_in_b"], np.float32)
    outb = np.asarray(inp["L1_attn_out_b"], np.float32)
    gam = np.asarray(inp["L1_attn_norm_gamma"], np.float32)
    bet = np.asarray(inp["L1_attn_norm_beta"], np.float32)

    def tilecols(v, nt):
        return np.ascontiguousarray(v.reshape(nt, 128).T)

    def tilemat(m, nt):
        W = m.shape[1]
        return np.ascontiguousarray(
            m.reshape(nt, 128, W).transpose(1, 0, 2).reshape(128, nt * W))
    return {
        "l1_inw": np.ascontiguousarray(
            inw.reshape(D, 48, 128).transpose(1, 0, 2)),
        "l1_outw": outw,
        "l1_Alog": tilemat(Alog, 16), "l1_Bp": tilemat(Bp, 16),
        "l1_Cp": tilemat(Cp, 16),
        "l1_Dp": tilecols(Dp, 16),
        "l1_convw": tilemat(convw, 16),
        "l1_convb": tilecols(convb, 16),
        "l1_inb": tilecols(inb, 48),
        "l1_outb": tilecols(outb, 8),
        "l1_gamma": tilecols(gam, 8),
        "l1_beta": tilecols(bet, 8),
    }


def run_l1_attn(x_by_core, weights_np):
    nc = build_l1_attn_program()
    consts = make_const_host()
    ident = make_ident_host()
    in_maps = []
    for c in range(8):
        m = dict(xT=x_by_core[c], const_in=consts, ident_in=ident)
        for k, v in weights_np.items():
            m[k + "_in" if k not in ("l1_inw", "l1_outw") else k] = v
        in_maps.append(m)
    res = run_bass_kernel_spmd(nc, in_maps, core_ids=list(range(8)))
    return res.results





def emit_l3_attn(ctx, x_bufs, xT, wdr, tag, fused=False):
    nc, prog = ctx.nc, ctx.prog
    ln_bf = [ctx.sb[f"bfA{i}"] for i in range(NDT)]
    ln_bufs = _bufnames("bfA", NDT)
    emit_layernorm(ctx, x_bufs, xT, ln_bf, ln_bufs, tag)
    if fused:
        for i in range(NDT):
            prog.dma(ctx.dram["l3_xsp"][i * 128:(i + 1) * 128, :], xT[i][:],
                     reads=[x_bufs[i]], writes=["l3xsp"], queue="sp",
                     tag=tag)

    PS = ctx.sb["PS"]
    scr = ctx.sb["scrA"]
    W = [ctx.sb.get(f"l3W{j}") for j in range(12)]
    Wn = [f"l3W{j}" for j in range(12)]

    d_u = ctx.dram["l3_u"]
    d_ubf = ctx.dram["l3_ubf"]
    d_zs = ctx.dram["l3_zs"]
    d_dt = ctx.dram["l3_dt"]
    d_dbf = ctx.dram["l3_dbf"]
    d_bc = ctx.dram["l3_bc"]
    d_y = ctx.dram["l3_y"]
    d_ybf = ctx.dram["l3_ybf"]

    wt = ctx.sb_w["l3_w"]
    for i in range(32):
        for k in range(NDT):
            prog.dma(wt[k][:], wdr["inpT_bf"][i][k * 128:(k + 1) * 128, :],
                     reads=["l3inpbf"], writes=["l3_w"], queue="sp", tag=tag)

        def mm(h):
            last = None
            for j in range(4):
                for k in range(NDT):
                    last = nc.tensor.matmul(
                        PS[:, j * 512:(j + 1) * 512], wt[k][:],
                        ln_bf[k][:, j * 512:(j + 1) * 512],
                        start=(k == 0), stop=(k == NDT - 1))
            return last
        prog.step("pe", mm, reads=list(ln_bufs) + ["l3_w"],
                  writes=["ps0", "ps1", "ps2", "ps3"], tag=tag + f"ip{i}")
        if i < 16:
            prog.step("dve",
                      lambda h: nc.vector.tensor_copy(out=W[4][:],
                                                      in_=PS[:, 0:TOK]),
                      reads=["ps0", "ps1", "ps2", "ps3"],
                      writes=[Wn[4], "ps0", "ps1", "ps2", "ps3"], tag=tag)

            def conv(h, i=i):
                cw = ctx.sb["l3_convw"]
                nc.vector.tensor_scalar(
                    out=W[5][:], in0=W[4][:],
                    scalar1=cw[:, i * 4 + 3:i * 4 + 4],
                    scalar2=ctx.sb["l3_convb"][:, i:i + 1],
                    op0=ALU.mult, op1=ALU.add)
                for tp in range(3):
                    sh = 3 - tp
                    nc.vector.scalar_tensor_tensor(
                        out=W[5][:, sh:TOK], in0=W[4][:, 0:TOK - sh],
                        scalar=cw[:, i * 4 + tp:i * 4 + tp + 1],
                        in1=W[5][:, sh:TOK], op0=ALU.mult, op1=ALU.add)
                return nc.vector.engine_nop()
            prog.step("dve", conv,
                      reads=[Wn[4], Wn[5], "l3_convw", "l3_convb"],
                      writes=[Wn[5]], tag=tag)
            prog.step("act",
                      lambda h: nc.scalar.activation(
                          out=W[4][:], in_=W[5][:], func=AF.Silu),
                      reads=[Wn[5]], writes=[Wn[4]], tag=tag)
            prog.dma(d_u[i], W[4][:], reads=[Wn[4]], writes=["l3ud"],
                     queue="sp", tag=tag)
            prog.step("dve",
                      lambda h: nc.vector.tensor_copy(
                          out=ctx.sb["l3_bfx"][:], in_=W[4][:]),
                      reads=[Wn[4]], writes=["l3_bfx"], tag=tag)
            prog.dma(d_ubf[i], ctx.sb["l3_bfx"][:], reads=["l3_bfx"],
                     writes=["l3ubfd"], queue="sp", tag=tag)
        else:
            prog.step("act",
                      lambda h: nc.scalar.activation(
                          out=W[4][:], in_=PS[:, 0:TOK], func=AF.Silu),
                      reads=["ps0", "ps1", "ps2", "ps3"],
                      writes=[Wn[4], "ps0", "ps1", "ps2", "ps3"], tag=tag)
            prog.dma(d_zs[i - 16], W[4][:], reads=[Wn[4]], writes=["l3zsd"],
                     queue="sp", tag=tag)

    for o in range(17):
        ow = 128 if o < 16 else 64
        for k in range(16):
            prog.dma(wt[k][:, 0:ow],
                     wdr["xpT_bf"][o][k * 128:(k + 1) * 128, 0:ow],
                     reads=["l3xpbf"], writes=["l3_w"], queue="sp", tag=tag)
            prog.dma(ctx.sb["l3_bfx"][:], d_ubf[k], reads=["l3ubfd"],
                     writes=["l3_bfx"], queue="actq", tag=tag)

            def mm(h, k=k, ow=ow):
                last = None
                for j in range(4):
                    last = nc.tensor.matmul(
                        PS[0:ow, j * 512:(j + 1) * 512], wt[k][:, 0:ow],
                        ctx.sb["l3_bfx"][:, j * 512:(j + 1) * 512],
                        start=(k == 0), stop=(k == 15))
                return last
            prog.step("pe", mm, reads=["l3_w", "l3_bfx"],
                      writes=["ps0", "ps1", "ps2", "ps3"],
                      tag=tag + f"sp{o}.{k}")
        if o < 16:
            prog.step("dve",
                      lambda h: nc.vector.tensor_copy(
                          out=ctx.sb["l3_bfx"][:], in_=PS[:, 0:TOK]),
                      reads=["ps0", "ps1", "ps2", "ps3"],
                      writes=["l3_bfx", "ps0", "ps1", "ps2", "ps3"], tag=tag)
            prog.dma(d_dbf[o], ctx.sb["l3_bfx"][:], reads=["l3_bfx"],
                     writes=["l3dbfd"], queue="sp", tag=tag)
        else:
            prog.step("dve",
                      lambda h: nc.vector.tensor_copy(
                          out=W[4][0:64, :], in_=PS[0:64, 0:TOK]),
                      reads=["ps0", "ps1", "ps2", "ps3"],
                      writes=[Wn[4], "ps0", "ps1", "ps2", "ps3"], tag=tag)
            prog.dma(d_bc, W[4][0:64, :], reads=[Wn[4]], writes=["l3bcd"],
                     queue="sp", tag=tag)

    for o in range(16):
        for k in range(16):
            prog.dma(wt[k][:], wdr["dtpT_bf"][o][k * 128:(k + 1) * 128, :],
                     reads=["l3dtpbf"], writes=["l3_w"], queue="sp", tag=tag)
            prog.dma(ctx.sb["l3_bfx"][:], d_dbf[k], reads=["l3dbfd"],
                     writes=["l3_bfx"], queue="actq", tag=tag)

            def mm(h, k=k):
                last = None
                for j in range(4):
                    last = nc.tensor.matmul(
                        PS[:, j * 512:(j + 1) * 512], wt[k][:],
                        ctx.sb["l3_bfx"][:, j * 512:(j + 1) * 512],
                        start=(k == 0), stop=(k == 15))
                return last
            prog.step("pe", mm, reads=["l3_w", "l3_bfx"],
                      writes=["ps0", "ps1", "ps2", "ps3"],
                      tag=tag + f"dt{o}.{k}")
        def sp_a(h, o=o):
            return nc.vector.tensor_scalar(
                out=W[4][:], in0=PS[:, 0:TOK],
                scalar1=ctx.sb["l3_dtb"][:, o:o + 1], scalar2=None,
                op0=ALU.add)
        prog.step("dve", sp_a,
                  reads=["ps0", "ps1", "ps2", "ps3", "l3_dtb"],
                  writes=[Wn[4], "ps0", "ps1", "ps2", "ps3"], tag=tag)
        prog.step("act", lambda h: nc.scalar.activation(
            out=W[5][:], in_=W[4][:], func=AF.Abs),
            reads=[Wn[4]], writes=[Wn[5]], tag=tag)
        prog.step("act", lambda h: nc.scalar.activation(
            out=W[5][:], in_=W[5][:], func=AF.Exp, scale=-1.0),
            reads=[Wn[5]], writes=[Wn[5]], tag=tag)
        prog.step("act", lambda h: nc.scalar.activation(
            out=W[5][:], in_=W[5][:], func=AF.Ln,
            bias=ccol(ctx, "one")),
            reads=[Wn[5], "consts"], writes=[Wn[5]], tag=tag)

        def sp_b(h):
            nc.vector.tensor_scalar_max(W[4][:], W[4][:], 0.0)
            return nc.vector.tensor_add(W[4][:], W[4][:], W[5][:])
        prog.step("dve", sp_b, reads=[Wn[4], Wn[5]], writes=[Wn[4]], tag=tag)
        prog.dma(d_dt[o], W[4][:], reads=[Wn[4]], writes=["l3dtd"],
                 queue="sp", tag=tag)

    ALr = ctx.sb["l3_Alr"]
    ALi = ctx.sb["l3_Ali"]
    Ar = ctx.sb["l3_Ar"]
    Ai = ctx.sb["l3_Ai"]

    for fn, rd, wr in (
            (lambda: nc.vector.tensor_scalar_mul(Ar[:], ALi[:],
                                                 1.0 / (2 * np.pi)),
             ["l3_Ali"], ["l3_Ar"]),
            (lambda: nc.vector.tensor_copy(out=ctx.sb["l3_Aq"][:],
                                           in_=Ar[:]),
             ["l3_Ar"], ["l3_Aq"]),
            (lambda: nc.vector.tensor_copy(out=Ar[:],
                                           in_=ctx.sb["l3_Aq"][:]),
             ["l3_Aq"], ["l3_Ar"]),
            (lambda: nc.vector.scalar_tensor_tensor(
                out=Ai[:], in0=Ar[:], scalar=-2 * np.pi, in1=ALi[:],
                op0=ALU.mult, op1=ALU.add),
             ["l3_Ar", "l3_Ali"], ["l3_Ai"])):
        prog.step("dve", lambda h, fn=fn: fn(), reads=rd, writes=wr,
                  tag=tag, sync_after=True)
    prog.step("act", lambda h: nc.scalar.activation(
        out=ctx.sb["l3_Ae"][:], in_=ALr[:], func=AF.Exp),
        reads=["l3_Alr"], writes=["l3_Ae"], tag=tag)
    prog.step("act", lambda h: nc.scalar.activation(
        out=ctx.sb["l3_As"][:], in_=Ai[:], func=AF.Sin),
        reads=["l3_Ai"], writes=["l3_As"], tag=tag)

    for fn, rd, wr in (
            (lambda: nc.vector.tensor_scalar_add(Ar[:], Ai[:], np.pi / 2),
             ["l3_Ai"], ["l3_Ar"]),
            (lambda: nc.vector.tensor_single_scalar(
                out=ctx.sb["l3_Aqf"][:], in_=Ar[:], scalar=float(np.pi),
                op=ALU.is_gt),
             ["l3_Ar"], ["l3_Aqf"]),
            (lambda: nc.vector.scalar_tensor_tensor(
                out=Ar[:], in0=ctx.sb["l3_Aqf"][:], scalar=-2 * np.pi,
                in1=Ar[:], op0=ALU.mult, op1=ALU.add),
             ["l3_Aqf", "l3_Ar"], ["l3_Ar"])):
        prog.step("dve", lambda h, fn=fn: fn(), reads=rd, writes=wr,
                  tag=tag, sync_after=True)
    prog.step("act", lambda h: nc.scalar.activation(
        out=Ar[:], in_=Ar[:], func=AF.Sin),
        reads=["l3_Ar"], writes=["l3_Ar"], tag=tag)

    for fn, rd, wr in (
            (lambda: nc.vector.tensor_mul(Ar[:], Ar[:], ctx.sb["l3_Ae"][:]),
             ["l3_Ar", "l3_Ae"], ["l3_Ar"]),
            (lambda: nc.vector.tensor_scalar_mul(Ar[:], Ar[:], -1.0),
             ["l3_Ar"], ["l3_Ar"]),
            (lambda: nc.vector.tensor_mul(Ai[:], ctx.sb["l3_As"][:],
                                          ctx.sb["l3_Ae"][:]),
             ["l3_As", "l3_Ae"], ["l3_Ai"]),
            (lambda: nc.vector.tensor_scalar_mul(Ai[:], Ai[:], -1.0),
             ["l3_Ai"], ["l3_Ai"])):
        prog.step("dve", lambda h, fn=fn: fn(), reads=rd, writes=wr,
                  tag=tag, sync_after=True)

    TP = float(2 * np.pi)
    HT = TOK // 2
    WA, WB = W[10], W[11]
    for n in range(16):
        for bi, nm in ((0, "l3_Brb"), (16, "l3_Bib"), (32, "l3_Crb"),
                       (48, "l3_Cib")):
            prog.dma(ctx.sb[nm][:],
                     d_bc[bi + n:bi + n + 1, :].broadcast_to([128, TOK]),
                     reads=["l3bcd"], writes=[nm], queue="sp", tag=tag)
        for i in range(16):
            arc = Ar[:, i * 16 + n:i * 16 + n + 1]
            aic = Ai[:, i * 16 + n:i * 16 + n + 1]
            prog.dma(ctx.sb["l3_ut"][:], d_u[i], reads=["l3ud"],
                     writes=["l3_ut"], queue="sp", tag=tag)
            prog.dma(ctx.sb["l3_dtt"][:], d_dt[i], reads=["l3dtd"],
                     writes=["l3_dtt"], queue="sp", tag=tag)
            if n > 0:
                prog.dma(ctx.sb["l3_yt"][:], d_y[i], reads=[f"l3yd{i}"],
                         writes=["l3_yt"], queue="actq", tag=tag)

            ut, dtt, yt = ctx.sb["l3_ut"], ctx.sb["l3_dtt"], ctx.sb["l3_yt"]
            du = ctx.sb["l3_du"]
            v = nc.vector
            g_ = nc.gpsimd
            s = nc.scalar

            def HV(j, hl):
                return W[j][0:128, hl * HT:(hl + 1) * HT]

            def HN(j, hl):
                return f"l3W{j}h{hl}"

            def HA(hl):
                return WA[0:128, hl * HT:(hl + 1) * HT]

            def HB(hl):
                return WB[0:128, hl * HT:(hl + 1) * HT]

            def SL(hl):
                return slice(hl * HT, (hl + 1) * HT)

            def stA(hl):
                def f(h, hl=hl, arc=arc, aic=aic):
                    sl = SL(hl)
                    v.tensor_mul(du[:, sl], dtt[:, sl], ut[:, sl])
                    v.tensor_scalar_mul(HV(0, hl), dtt[:, sl], arc)
                    return v.tensor_scalar_mul(HV(1, hl), dtt[:, sl], aic)
                prog.step("dve", f,
                          reads=["l3_dtt", "l3_ut", "l3_Ar", "l3_Ai"],
                          writes=[f"l3_du{hl}", HN(0, hl), HN(1, hl)],
                          tag=tag)

            def stB(hl):
                def f(h, hl=hl):
                    s.activation(out=HV(2, hl), in_=HV(0, hl),
                                 func=AF.Square)
                    return s.activation(out=HV(3, hl), in_=HV(1, hl),
                                        func=AF.Square)
                prog.step("act", f, reads=[HN(0, hl), HN(1, hl)],
                          writes=[HN(2, hl), HN(3, hl)], tag=tag)

            def stC(hl):
                def f(h, hl=hl):
                    v.tensor_add(HV(2, hl), HV(2, hl), HV(3, hl))
                    v.scalar_tensor_tensor(
                        out=HV(3, hl), in0=HV(0, hl), scalar=4.0,
                        in1=HV(2, hl), op0=ALU.mult, op1=ALU.add)
                    return v.scalar_tensor_tensor(
                        out=HV(4, hl), in0=HV(0, hl), scalar=-4.0,
                        in1=HV(2, hl), op0=ALU.mult, op1=ALU.add)
                prog.step("dve", f, reads=[HN(0, hl), HN(2, hl), HN(3, hl)],
                          writes=[HN(2, hl), HN(3, hl), HN(4, hl)], tag=tag)

            def stD(hl):
                def f(h, hl=hl):
                    s.activation(out=HV(3, hl), in_=HV(3, hl), func=AF.Ln,
                                 bias=ccol(ctx, "four"))
                    return s.activation(out=HV(4, hl), in_=HV(4, hl),
                                        func=AF.Ln, bias=ccol(ctx, "four"))
                prog.step("act", f, reads=[HN(3, hl), HN(4, hl), "consts"],
                          writes=[HN(3, hl), HN(4, hl)], tag=tag)

            def stE(hl):
                def f(h, hl=hl):
                    v.tensor_sub(HV(3, hl), HV(3, hl), HV(4, hl))
                    return v.tensor_scalar(
                        out=HV(5, hl), in0=HV(2, hl), scalar1=-1.0,
                        scalar2=4.0, op0=ALU.mult, op1=ALU.add)
                prog.step("dve", f, reads=[HN(2, hl), HN(3, hl), HN(4, hl)],
                          writes=[HN(3, hl), HN(5, hl)], tag=tag)

            def stF(hl):
                def f(h, hl=hl):
                    s.activation(out=HV(4, hl), in_=HV(4, hl), func=AF.Exp,
                                 scale=-1.0, bias=ccol(ctx, "ln2"))
                    s.activation(out=HV(6, hl), in_=HV(5, hl),
                                 func=AF.Square)
                    s.activation(out=HV(6, hl), in_=HV(6, hl), func=AF.Ln)
                    s.activation(out=HV(6, hl), in_=HV(6, hl), func=AF.Exp,
                                 scale=-1.0)
                    return s.activation(out=HV(7, hl), in_=HV(5, hl),
                                        func=AF.Sign)
                prog.step("act", f, reads=[HN(4, hl), HN(5, hl), "consts"],
                          writes=[HN(4, hl), HN(6, hl), HN(7, hl)], tag=tag)

            def stP(hl):
                def f(h, hl=hl):
                    sl = SL(hl)
                    brb = ctx.sb["l3_Brb"][:, sl]
                    bib = ctx.sb["l3_Bib"][:, sl]
                    g_.tensor_mul(HV(4, hl), HV(4, hl), du[:, sl])
                    g_.tensor_tensor(
                        out=HA(hl),
                        in0=ccol(ctx, "two").broadcast_to([128, HT]),
                        in1=HV(0, hl), op=ALU.subtract)
                    g_.tensor_mul(HB(hl), brb, HA(hl))
                    g_.tensor_mul(HV(0, hl), bib, HV(1, hl))
                    g_.tensor_add(HB(hl), HB(hl), HV(0, hl))
                    g_.tensor_mul(HB(hl), HB(hl), HV(4, hl))
                    g_.tensor_mul(HA(hl), bib, HA(hl))
                    g_.tensor_mul(HV(0, hl), brb, HV(1, hl))
                    g_.tensor_sub(HA(hl), HA(hl), HV(0, hl))
                    return g_.tensor_mul(HA(hl), HA(hl), HV(4, hl))
                prog.step("pool", f,
                          reads=[HN(0, hl), HN(1, hl), HN(4, hl),
                                 f"l3_du{hl}", "l3_Brb", "l3_Bib", "consts"],
                          writes=[HN(0, hl), HN(4, hl), f"l3WAh{hl}",
                                  f"l3WBh{hl}"], tag=tag)

            def stG(hl):
                def f(h, hl=hl):
                    v.tensor_mul(HV(2, hl), HV(1, hl), HV(5, hl))
                    return v.scalar_tensor_tensor(
                        out=HV(2, hl), in0=HV(2, hl), scalar=4.0,
                        in1=HV(6, hl), op0=ALU.mult, op1=ALU.mult)
                prog.step("dve", f, reads=[HN(1, hl), HN(5, hl), HN(6, hl)],
                          writes=[HN(2, hl)], tag=tag)

            def stH(hl):
                def f(h, hl=hl):
                    s.activation(out=HV(3, hl), in_=HV(3, hl), func=AF.Exp,
                                 scale=0.5)
                    return s.activation(out=HV(2, hl), in_=HV(2, hl),
                                        func=AF.Arctan)
                prog.step("act", f, reads=[HN(2, hl), HN(3, hl)],
                          writes=[HN(2, hl), HN(3, hl)], tag=tag)

            def stI(hl):
                def f(h, hl=hl):
                    sl = SL(hl)
                    v.tensor_scalar(
                        out=HV(5, hl), in0=HV(7, hl),
                        scalar1=float(-np.pi / 2), scalar2=float(np.pi / 2),
                        op0=ALU.mult, op1=ALU.add)
                    v.tensor_add(HV(2, hl), HV(2, hl), HV(5, hl))
                    ini = 0.0 if hl == 0 else W[5][0:128, HT - 1:HT]
                    v.tensor_tensor_scan(
                        out=HV(5, hl),
                        data0=ctx.sb["ones_f32"][:, 0:1].broadcast_to(
                            [128, HT]),
                        data1=HV(2, hl), initial=ini,
                        op0=ALU.mult, op1=ALU.add)
                    v.tensor_scalar_mul(HV(6, hl), HV(5, hl), 1.0 / TP)
                    v.tensor_copy(out=ctx.sb["l3_qi"][:, sl], in_=HV(6, hl))
                    v.tensor_copy(out=HV(6, hl), in_=ctx.sb["l3_qi"][:, sl])
                    return v.scalar_tensor_tensor(
                        out=HV(6, hl), in0=HV(6, hl), scalar=-TP,
                        in1=HV(5, hl), op0=ALU.mult, op1=ALU.add)
                rds = [HN(2, hl), HN(7, hl), "ones_f32", "scrA"]
                if hl == 1:
                    rds.append(HN(5, 0))
                prog.step("dve", f, reads=rds,
                          writes=[HN(2, hl), HN(5, hl), HN(6, hl), "scrA"],
                          tag=tag)

            def stJ(hl):
                def f(h, hl=hl):
                    s.activation(out=HV(7, hl), in_=HV(6, hl), func=AF.Sin,
                                 scale=0.5)
                    s.activation(out=HV(6, hl), in_=HV(6, hl), func=AF.Sin,
                                 scale=0.5, bias=ccol(ctx, "pihalf"))
                    return s.activation(out=HV(5, hl), in_=HV(7, hl),
                                        func=AF.Square)
                prog.step("act", f, reads=[HN(6, hl), "consts"],
                          writes=[HN(5, hl), HN(6, hl), HN(7, hl)], tag=tag)

            def stK(hl):
                def f(h, hl=hl):
                    v.tensor_scalar(out=HV(5, hl), in0=HV(5, hl),
                                    scalar1=-2.0, scalar2=1.0,
                                    op0=ALU.mult, op1=ALU.add)
                    return v.scalar_tensor_tensor(
                        out=HV(7, hl), in0=HV(7, hl), scalar=2.0,
                        in1=HV(6, hl), op0=ALU.mult, op1=ALU.mult)
                prog.step("dve", f, reads=[HN(5, hl), HN(6, hl), HN(7, hl)],
                          writes=[HN(5, hl), HN(7, hl)], tag=tag)

            def stL(hl):
                def f(h, hl=hl):
                    v.tensor_mul(HV(0, hl), HV(5, hl), HB(hl))
                    v.tensor_mul(HV(6, hl), HV(7, hl), HA(hl))
                    v.tensor_add(HV(0, hl), HV(0, hl), HV(6, hl))
                    v.tensor_mul(HV(6, hl), HV(5, hl), HA(hl))
                    v.tensor_mul(HV(1, hl), HV(7, hl), HB(hl))
                    return v.tensor_sub(HV(1, hl), HV(6, hl), HV(1, hl))
                prog.step("dve", f,
                          reads=[HN(5, hl), HN(7, hl), f"l3WAh{hl}",
                                 f"l3WBh{hl}"],
                          writes=[HN(0, hl), HN(1, hl), HN(6, hl)], tag=tag)

            def stM(hl):
                def f(h, hl=hl):
                    ini_r = 0.0 if hl == 0 else W[6][0:128, HT - 1:HT]
                    ini_i = 0.0 if hl == 0 else W[4][0:128, HT - 1:HT]
                    v.tensor_tensor_scan(
                        out=HV(6, hl), data0=HV(3, hl), data1=HV(0, hl),
                        initial=ini_r, op0=ALU.mult, op1=ALU.add)
                    return v.tensor_tensor_scan(
                        out=HV(4, hl), data0=HV(3, hl), data1=HV(1, hl),
                        initial=ini_i, op0=ALU.mult, op1=ALU.add)
                rds = [HN(0, hl), HN(1, hl), HN(3, hl)]
                if hl == 1:
                    rds += [HN(6, 0), HN(4, 0)]
                prog.step("dve", f, reads=rds,
                          writes=[HN(4, hl), HN(6, hl)], tag=tag)

            def stN(hl):
                def f(h, hl=hl, n=n):
                    sl = SL(hl)
                    crb = ctx.sb["l3_Crb"][:, sl]
                    cib = ctx.sb["l3_Cib"][:, sl]
                    v.tensor_mul(HV(0, hl), crb, HV(5, hl))
                    v.tensor_mul(HV(1, hl), cib, HV(7, hl))
                    v.tensor_sub(HV(0, hl), HV(0, hl), HV(1, hl))
                    v.tensor_mul(HV(1, hl), crb, HV(7, hl))
                    v.tensor_mul(HV(2, hl), cib, HV(5, hl))
                    v.tensor_add(HV(1, hl), HV(1, hl), HV(2, hl))
                    v.tensor_mul(HV(2, hl), HV(6, hl), HV(0, hl))
                    v.tensor_mul(HV(3, hl), HV(4, hl), HV(1, hl))
                    v.tensor_sub(HV(2, hl), HV(2, hl), HV(3, hl))
                    if n == 0:
                        return v.tensor_copy(out=yt[:, sl], in_=HV(2, hl))
                    return v.tensor_add(yt[:, sl], yt[:, sl], HV(2, hl))
                rds = ["l3_Crb", "l3_Cib", HN(4, hl), HN(5, hl),
                       HN(6, hl), HN(7, hl)]
                if n > 0:
                    rds.append("l3_yt")
                prog.step("dve", f, reads=rds,
                          writes=[HN(0, hl), HN(1, hl), HN(2, hl),
                                  HN(3, hl), "l3_yt"], tag=tag)

            for st in (stA, stB, stC, stD, stE, stF, stP, stG, stH, stI,
                       stJ, stK, stL, stM, stN):
                st(0)
                st(1)

            if n < 15:
                prog.dma(d_y[i], yt[:], reads=["l3_yt"], writes=[f"l3yd{i}"],
                         queue="sp", tag=tag)
            else:
                prog.dma(ctx.sb["l3_du"][:], d_zs[i], reads=["l3zsd"],
                         writes=["l3_du0", "l3_du1"], queue="actq", tag=tag)

                def fin(h, i=i):
                    v.scalar_tensor_tensor(
                        out=yt[:], in0=ut[:],
                        scalar=ctx.sb["l3_D"][:, i:i + 1], in1=yt[:],
                        op0=ALU.mult, op1=ALU.add)
                    v.tensor_mul(yt[:], yt[:], ctx.sb["l3_du"][:])
                    return v.tensor_copy(out=ctx.sb["l3_bfx"][:], in_=yt[:])
                prog.step("dve", fin,
                          reads=["l3_yt", "l3_ut", "l3_D", "l3_du0",
                                 "l3_du1"],
                          writes=["l3_yt", "l3_bfx"], tag=tag)
                prog.dma(d_ybf[i], ctx.sb["l3_bfx"][:], reads=["l3_bfx"],
                         writes=["l3ybfd"], queue="sp", tag=tag)

    prog.fence()
    if fused:
        xin_ap = ctx.dram["l3_xsp"]
        xout_ap = None
    else:
        xin_ap = ctx.dram["l3_xin"]
        xout_ap = ctx.dram["l3_xout"]
    for c in range(4):
        for k in range(16):
            prog.dma(ctx.sb_w["l3_ow"][0][:],
                     wdr["outT_bf"][k * 128:(k + 1) * 128, :],
                     reads=["l3outbf"], writes=["l3_ow"], queue="sp", tag=tag)
            prog.dma(ctx.sb["l3_bfx"][:, 0:512],
                     d_ybf[k][0:128, c * 512:(c + 1) * 512],
                     reads=["l3ybfd"], writes=["l3_bfx"], queue="actq",
                     tag=tag)

            def mm_o(h, k=k, c=c):
                w2t = ctx.sb_w["l3_ow"][0]
                last = None
                for o in range(NDT):
                    last = nc.tensor.matmul(
                        ctx.psb(o, 1),
                        w2t[:, o * 128:(o + 1) * 128],
                        ctx.sb["l3_bfx"][:, 0:512],
                        start=(k == 0), stop=(k == 15))
                return last
            prog.step("pe", mm_o, reads=["l3_bfx", "l3_ow"],
                      writes=[f"ps{o}" for o in range(NDT)],
                      tag=tag + f"o{c}.{k}")
        for o in range(NDT):
            prog.dma(W[0][:, 0:512],
                     xin_ap[o * 128:(o + 1) * 128, c * 512:(c + 1) * 512],
                     reads=["l3xsp"] if fused else (),
                     writes=[Wn[0]], queue="sp", tag=tag)
            if fused:
                def evf(h, o=o, c=c):
                    return nc.vector.tensor_add(
                        xT[o][:, c * 512:(c + 1) * 512], ctx.psb(o, 1),
                        W[0][:, 0:512])
                prog.step("dve", evf, reads=[f"ps{o}", Wn[0]],
                          writes=[x_bufs[o], f"ps{o}"], tag=tag + f"ev{c}")
            else:
                def ev(h, o=o, c=c):
                    return nc.vector.tensor_add(
                        W[1][:, 0:512], ctx.psb(o, 1), W[0][:, 0:512])
                prog.step("dve", ev, reads=[f"ps{o}", Wn[0]],
                          writes=[Wn[1], f"ps{o}"], tag=tag + f"ev{c}")
                prog.dma(xout_ap[o * 128:(o + 1) * 128,
                                 c * 512:(c + 1) * 512],
                         W[1][:, 0:512], reads=[Wn[1]], writes=["l3xo"],
                         queue="sp", tag=tag)


def build_l3_attn_program():
    nc = bass.Bass("TRN2", target_bir_lowering=False)
    xin, consts_in, ident_in, xout = _mk_base(nc)
    w_in = {
        "l3_inpT": nc.dram_tensor("l3_inpT", [32, D, 128], F32,
                                  kind="ExternalInput").ap(),
        "l3_xpT": nc.dram_tensor("l3_xpT", [17, 2048, 128], F32,
                                 kind="ExternalInput").ap(),
        "l3_dtpT": nc.dram_tensor("l3_dtpT", [16, 2048, 128], F32,
                                  kind="ExternalInput").ap(),
        "l3_outT": nc.dram_tensor("l3_outT", [2048, D], F32,
                                  kind="ExternalInput").ap(),
        "l3_Alr": nc.dram_tensor("l3_Alr_in", [128, 256], F32,
                                 kind="ExternalInput").ap(),
        "l3_Ali": nc.dram_tensor("l3_Ali_in", [128, 256], F32,
                                 kind="ExternalInput").ap(),
        "l3_D": nc.dram_tensor("l3_D_in", [128, 16], F32,
                               kind="ExternalInput").ap(),
        "l3_convw": nc.dram_tensor("l3_convw_in", [128, 64], F32,
                                   kind="ExternalInput").ap(),
        "l3_convb": nc.dram_tensor("l3_convb_in", [128, 16], F32,
                                   kind="ExternalInput").ap(),
        "l3_dtb": nc.dram_tensor("l3_dtb_in", [128, 16], F32,
                                 kind="ExternalInput").ap(),
    }
    with contextlib.ExitStack() as stack:
        prog = Prog(nc)
        ctx = Ctx(nc, prog, stack)
        alloc_common(ctx, bf_groups=0)
        for j in range(8):
            ctx.sbuf(f"l3W{j}", [128, TOK], F32)
        ctx.sb["l3W8"] = None
        ctx.sb["l3W9"] = None
        ctx.sbuf("l3W10", [128, TOK], F32)
        ctx.sbuf("l3W11", [128, TOK], F32)
        for i in range(NDT):
            wj = ctx.sb[f"l3W{i // 2}"]
            bb = wj.ap().bitcast(BF16) if hasattr(wj, "ap") else None
            ctx.sb[f"bfA{i}"] = wj[0:128, 0:TOK].bitcast(BF16)[
                0:128, (i % 2) * TOK:(i % 2 + 1) * TOK]
        for nm, xi in (("l3_Brb", 0), ("l3_Bib", 1), ("l3_Crb", 2),
                       ("l3_Cib", 3), ("l3_ut", 4), ("l3_dtt", 5),
                       ("l3_yt", 6), ("l3_du", 7)):
            ctx.sb[nm] = ctx.sb[f"xT{xi}"]
        ctx.sb["l3_qi"] = ctx.sb["scrA"][0:128, 0:TOK].bitcast(I32)
        ctx.sbuf("l3_bfx", [128, TOK], BF16)
        for nm, shape in (("l3_Alr", [128, 256]), ("l3_Ali", [128, 256]),
                          ("l3_Ar", [128, 256]), ("l3_Ai", [128, 256]),
                          ("l3_Ae", [128, 256]), ("l3_As", [128, 256]),
                          ("l3_Aq", [128, 256]), ("l3_D", [128, 16]),
                          ("l3_convw", [128, 64]), ("l3_convb", [128, 16]),
                          ("l3_dtb", [128, 16])):
            if nm == "l3_Aq":
                ctx.sbuf(nm, shape, I32)
                ctx.sbuf("l3_Aqf", shape, F32)
            else:
                ctx.sbuf(nm, shape, F32)
        for nm in ("l3_Alr", "l3_Ali", "l3_D", "l3_convw", "l3_convb",
                   "l3_dtb"):
            prog.dma(ctx.sb[nm][:], w_in[nm], writes=[nm])
        ctx.sb_w["l3_w"] = [ctx.sbuf(f"l3_w_{k}", [128, 128], BF16)
                            for k in range(16)]
        ctx.sb_w["l3_ow"] = [ctx.sbuf("l3_ow_0", [128, 1024], BF16)]
        ctx.sb["castb"] = ctx.sb["l3_bfx"][0:128, 0:512]
        ctx.sb["castf"] = ctx.sb["l3W11"]
        for nm, shape, dt_ in (("l3_u", [16, 128, TOK], F32),
                               ("l3_ubf", [16, 128, TOK], BF16),
                               ("l3_zs", [16, 128, TOK], F32),
                               ("l3_dt", [16, 128, TOK], F32),
                               ("l3_dbf", [16, 128, TOK], BF16),
                               ("l3_bc", [64, TOK], F32),
                               ("l3_y", [16, 128, TOK], F32),
                               ("l3_ybf", [16, 128, TOK], BF16)):
            ctx.dram[nm] = nc.dram_tensor(nm + "_scr", shape, dt_,
                                          kind="Internal").ap()
        _setup_common(ctx, xin, consts_in, ident_in)

        wdr = {}
        inpbf = nc.dram_tensor("l3_inpT_bf", [32, D, 128], BF16,
                               kind="Internal").ap()
        for t in range(32):
            precast_to_dram(ctx, "l3inpbf", w_in["l3_inpT"][t], inpbf[t],
                            D, 128)
        xpbf = nc.dram_tensor("l3_xpT_bf", [17, 2048, 128], BF16,
                              kind="Internal").ap()
        for t in range(17):
            precast_to_dram(ctx, "l3xpbf", w_in["l3_xpT"][t], xpbf[t],
                            2048, 128)
        dtpbf = nc.dram_tensor("l3_dtpT_bf", [16, 2048, 128], BF16,
                               kind="Internal").ap()
        for t in range(16):
            precast_to_dram(ctx, "l3dtpbf", w_in["l3_dtpT"][t], dtpbf[t],
                            2048, 128)
        outbf = nc.dram_tensor("l3_outT_bf", [2048, D], BF16,
                               kind="Internal").ap()
        precast_to_dram(ctx, "l3outbf", w_in["l3_outT"], outbf, 2048, D)
        wdr = {"inpT_bf": inpbf, "xpT_bf": xpbf, "dtpT_bf": dtpbf,
               "outT_bf": outbf}

        ctx.dram["l3_xin"] = xin
        ctx.dram["l3_xout"] = xout
        xT = [ctx.sb[f"xT{i}"] for i in range(NDT)]
        emit_l3_attn(ctx, _bufnames("xT", NDT), xT, wdr, "l3a_")
        prog.finalize(stack)
    return nc


def stage_l3_attn_weights(inp):
    inp_w = np.asarray(inp["L3_attn_in_proj_w"], np.float32)
    xp = np.asarray(inp["L3_attn_x_proj_w"], np.float32)
    dtp = np.asarray(inp["L3_attn_dt_proj_w"], np.float32)
    outp = np.asarray(inp["L3_attn_out_proj_w"], np.float32)
    Alr = np.asarray(inp["L3_attn_A_log_re"], np.float32)
    Ali = np.asarray(inp["L3_attn_A_log_im"], np.float32)
    Dd = np.asarray(inp["L3_attn_D"], np.float32)
    convw = np.asarray(inp["L3_attn_conv_w"], np.float32)[:, 0, :]
    convb = np.asarray(inp["L3_attn_conv_b"], np.float32)
    dtb = np.asarray(inp["L3_attn_dt_proj_b"], np.float32)

    def tilecols(v, nt):
        return np.ascontiguousarray(v.reshape(nt, 128).T)

    def tilemat(m, nt):
        Wd = m.shape[1]
        return np.ascontiguousarray(
            m.reshape(nt, 128, Wd).transpose(1, 0, 2).reshape(128, nt * Wd))

    def tmaj(mT, nt, width=128):
        K = mT.shape[0]
        return np.ascontiguousarray(
            mT.reshape(K, nt, width).transpose(1, 0, 2))

    inpT = inp_w.T
    xpT_full = xp.T
    xpT_pad = np.zeros((2048, 17 * 128), np.float32)
    xpT_pad[:, 0:2112] = xpT_full
    dtpT = dtp.T
    outT = outp.T
    return {
        "l3_inpT": tmaj(inpT, 32),
        "l3_xpT": tmaj(xpT_pad, 17),
        "l3_dtpT": tmaj(dtpT, 16),
        "l3_outT": outT,
        "l3_Alr": tilemat(Alr, 16),
        "l3_Ali": tilemat(Ali, 16),
        "l3_D": tilecols(Dd, 16),
        "l3_convw": tilemat(convw, 16),
        "l3_convb": tilecols(convb, 16),
        "l3_dtb": tilecols(dtb, 16),
    }


def run_l3_attn(x_by_core, weights_np):
    nc = build_l3_attn_program()
    consts = make_const_host()
    ident = make_ident_host()
    in_maps = []
    for c in range(8):
        m = dict(xT=x_by_core[c], const_in=consts, ident_in=ident)
        for k, v in weights_np.items():
            m[k + "_in" if k in ("l3_Alr", "l3_Ali", "l3_D", "l3_convw",
                                 "l3_convb", "l3_dtb") else k] = v
        in_maps.append(m)
    res = run_bass_kernel_spmd(nc, in_maps, core_ids=list(range(8)))
    return res.results



def build_fused_program():
    nc = bass.Bass("TRN2", target_bir_lowering=False)
    xin = nc.dram_tensor("xT", [D, TOK], F32, kind="ExternalInput").ap()
    consts_in = nc.dram_tensor("const_in", [128, len(CONST_COLS)], F32,
                               kind="ExternalInput").ap()
    ident_in = nc.dram_tensor("ident_in", [128, 128], F32,
                              kind="ExternalInput").ap()
    xout = nc.dram_tensor("xT_out", [D, TOK], F32, kind="ExternalOutput").ap()

    wi = {}
    def din(nm, shape, dt_=F32):
        wi[nm] = nc.dram_tensor(nm, shape, dt_, kind="ExternalInput").ap()
        return wi[nm]

    din("l0_wg", [8, D, D]); din("l0_wu", [8, D, D]); din("l0_wd", [8, D, D])
    din("l0_bg_in", [128, 64]); din("l0_bu_in", [128, 64])
    din("l0_db_in", [8, D])
    din("l0_idx", [TOK, 2], I32); din("l0_rw", [TOK, 8])
    din("l0_wqT", [D, D]); din("l0_wkT", [D, D]); din("l0_wvT", [D, D])
    din("l0_woT", [D, D])
    for nm in ("l0_gq_in", "l0_bq_in", "l0_gk_in", "l0_bk_in"):
        din(nm, [128, 1])
    din("l0_bo_in", [128, NDT])
    din("mask4_in", [128, 512], BF16); din("ind2_l0", [128, 2])
    din("ind2_l2", [128, 2])
    din("l1_gwT", [D, 16])
    din("l1_w1", [16, D, 512]); din("l1_w3", [16, D, 512])
    din("l1_w2", [16, 512, D])
    din("l1_inw", [48, D, 128]); din("l1_outw", [2048, D])
    din("l1_Alog_in", [128, 1024]); din("l1_Bp_in", [128, 1024])
    din("l1_Cp_in", [128, 1024])
    din("l1_Dp_in", [128, 16]); din("l1_convw_in", [128, 48])
    din("l1_convb_in", [128, 16]); din("l1_inb_in", [128, 48])
    din("l1_outb_in", [128, 8]); din("l1_gamma_in", [128, 8])
    din("l1_beta_in", [128, 8])
    din("l2_wW", [D, D]); din("l2_wA1", [D, D]); din("l2_wA2", [D, D])
    din("l2_bias_in", [128, NDT])
    din("l2_wqT", [D, D]); din("l2_wkT", [D, D]); din("l2_wvT", [D, D])
    din("l2_woT", [D, D])
    din("l2_qw_in", [128, 1]); din("l2_kw_in", [128, 1])
    din("rope_cos_in", [128, TOK]); din("rope_sin_in", [128, TOK])
    din("l3_gwT", [D, 16])
    din("l3_w1T", [16, D, 512]); din("l3_w2T", [16, 512, D])
    din("l3_sfc1T", [16, D, 128]); din("l3_sfc2T", [16, D, 128])
    din("l3_sfc3T", [2048, D])
    din("l3_inpT", [32, D, 128]); din("l3_xpT", [17, 2048, 128])
    din("l3_dtpT", [16, 2048, 128]); din("l3_outT", [2048, D])
    din("l3_Alr_in", [128, 256]); din("l3_Ali_in", [128, 256])
    din("l3_D_in", [128, 16]); din("l3_convw_in", [128, 64])
    din("l3_convb_in", [128, 16]); din("l3_dtb_in", [128, 16])

    with contextlib.ExitStack() as stack:
        prog = Prog(nc)
        ctx = Ctx(nc, prog, stack)
        alloc_common(ctx, bf_groups=1)

        G = [ctx.sbuf(f"Gp{j}", [128, TOK], F32) for j in range(10)]
        Gbf = [g[0:128, 0:TOK].bitcast(BF16) for g in G]

        ctx.sbuf("castf0", [128, 512], F32)
        ctx.sbuf("castf1", [128, 512], F32)
        ctx.sbuf("castb0", [128, 512], BF16)
        ctx.sbuf("castb1", [128, 512], BF16)
        ctx.sb["castf2"] = G[9][0:128, 0:512]
        ctx.sb["castf3"] = G[9][0:128, 512:1024]
        ctx.sb["castb2"] = Gbf[9][0:128, 2048:2560]
        ctx.sb["castb3"] = Gbf[9][0:128, 2560:3072]

        wtu = [ctx.sbuf(f"wtu{k}", [128, 128], BF16) for k in range(16)]
        ctx.sbuf("mask4", [128, 512], BF16)
        ctx.sbuf("ind2", [128, 2], F32)
        ctx.sbuf("nm_col", [128, 4], F32)
        ctx.sbuf("VT0", [128, 1024], BF16)
        ctx.sbuf("VT1", [128, 1024], BF16)
        ctx.sbuf("attT4", [128, 512], BF16)
        ctx.sb_w["mha_wo_0"] = [ctx.sbuf("mha_wo", [128, 1024], BF16)]
        ctx.sb_w["mha_wo_1"] = ctx.sb_w["mha_wo_0"]

        for nm, shape in (("l0_gq", [128, 1]), ("l0_bq", [128, 1]),
                          ("l0_gk", [128, 1]), ("l0_bk", [128, 1]),
                          ("l0_bo", [128, NDT]), ("l2_bias", [128, NDT]),
                          ("l2_qw", [128, 1]), ("l2_kw", [128, 1])):
            ctx.sbuf(nm, shape, F32)
        vt1f = ctx.sb["VT1"][0:128, 0:1024].bitcast(F32)
        ctx.sb["l0_idxi"] = vt1f[0:128, 0:32].bitcast(I32)
        ctx.sb["l0_idxf"] = vt1f[0:128, 32:64]
        ctx.sb["l1_Dp"] = vt1f[0:128, 64:80]
        ctx.sb["l1_convw"] = vt1f[0:128, 80:128]
        ctx.sb["l1_convb"] = vt1f[0:128, 128:144]
        ctx.sb["l1_inb"] = vt1f[0:128, 144:192]
        ctx.sb["l1_outb"] = vt1f[0:128, 192:200]
        ctx.sb["l1_gamma"] = vt1f[0:128, 200:208]
        ctx.sb["l1_beta"] = vt1f[0:128, 208:216]
        ctx.sb["l3_D"] = vt1f[0:128, 216:232]
        ctx.sb["l3_convw"] = vt1f[0:128, 232:296]
        ctx.sb["l3_convb"] = vt1f[0:128, 296:312]
        ctx.sb["l3_dtb"] = vt1f[0:128, 312:328]
        ctx.sb["l0_bg"] = vt1f[0:128, 328:392]
        ctx.sb["l0_bu"] = vt1f[0:128, 392:456]
        ctx.sb["l0_rwt"] = ctx.sb["attT4"][0:128, 0:256].bitcast(F32)
        ctx.sb["r_rstdc"] = ctx.sb["castf0"][0:128, 0:512]
        ctx.sb["l3_Alr"] = ctx.sb["castf0"][0:128, 0:256]
        ctx.sb["l3_Ali"] = ctx.sb["castf0"][0:128, 256:512]
        ctx.sb["l3_Ar"] = ctx.sb["castf1"][0:128, 0:256]
        ctx.sb["l3_Ai"] = ctx.sb["castf1"][0:128, 256:512]
        ctx.sb["l3_Aq"] = ctx.sb["castb0"][0:128, 0:512].bitcast(I32)
        ctx.sb["l3_Aqf"] = ctx.sb["castb1"][0:128, 0:512].bitcast(F32)

        for i in range(32):
            ctx.sb[f"hall{i}"] = Gbf[i // 8][
                0:128, (i % 8) * 512:(i % 8 + 1) * 512]
        def btiles(gidxs, width):
            out = []
            for gi in gidxs:
                for t in range(4096 // width):
                    out.append(Gbf[gi][0:128, t * width:(t + 1) * width])
            return out
        ctx.sb_w["moe_w1_0"] = btiles([4, 5], 1024)
        ctx.sb_w["moe_w3_0"] = btiles([6, 7], 1024)
        ctx.sb_w["moe_w2_0"] = [Gbf[8][0:128, 0:1024]]
        ctx.sb_w["moe_w2_1"] = [Gbf[8][0:128, 1024:2048]]
        ctx.sb_w["sfc3_0"] = ctx.sb_w["moe_w2_0"]
        ctx.sb_w["sfc3_1"] = ctx.sb_w["moe_w2_1"]
        ctx.sb["score_rows"] = G[9][0:16, 0:TOK]
        ctx.sb["l0_db"] = G[9][64:72, 0:1024]
        ctx.sb["l0_sc64"] = G[9][64:72, 1024:1536]
        for i in range(NDT):
            ctx.sb[f"bfB{i}"] = Gbf[i // 2][
                0:128, (i % 2) * 2048:(i % 2 + 1) * 2048]
        ctx.sb_w["l2_wW"] = btiles([4, 5], 1024)
        ctx.sb_w["l2_wA1"] = btiles([6, 7], 1024)
        ctx.sb["v_f32"] = G[0]
        ctx.sb["q_bf"] = Gbf[1][0:128, 0:2048]
        ctx.sb["k_bf"] = Gbf[1][0:128, 2048:4096]
        ctx.sb["ropesw"] = G[3]
        for p in range(8):
            ctx.sb[f"aout{p}"] = Gbf[4 + p // 2][
                0:128, (p % 2) * 2048:(p % 2 + 1) * 2048]
        ctx.sb["rope_cs2"] = G[8]
        ctx.sb["rope_sn2"] = G[9]
        ctx.sb["qk_mub"] = G[3]
        ctx.sb["qk_rstdb"] = G[2]
        ctx.sb_w["mha_ws"] = wtu[0:8]
        ctx.sb_w["shsfc1_0"] = wtu[0:8]
        ctx.sb_w["shsfc2_0"] = wtu[8:16]
        ctx.sb["score_b"] = ctx.sb["VT0"][0:128, 0:1024].bitcast(F32)
        ctx.sb["sh_tmp"] = ctx.sb["attT4"][0:128, 0:256]
        ctx.sb["l1_u"] = G[0]
        ctx.sb["l1_uc"] = G[1]
        ctx.sb["l1_hd"] = G[2]
        ctx.sb["l1_hp"] = G[3]
        ctx.sb["l1_yd"] = G[4]
        ctx.sb["l1_yp"] = G[5]
        ctx.sb["l1_vs"] = G[6]
        ctx.sb["l1_gs"] = G[0]
        ctx.sb["l1_t2"] = G[0]
        ctx.sb["l1_ybf"] = Gbf[8][0:128, 0:2048]
        ctx.sb_w["l1_outw_0"] = [Gbf[8][0:128, 2048:3072]]
        ctx.sb_w["l1_outw_1"] = ctx.sb_w["l1_outw_0"]
        ctx.sb["l1_ptmp"] = G[7]
        ctx.sb["l1_a"] = G[9][0:128, 0:1024]
        ctx.sb["l1_gsc"] = G[9][0:128, 1024:2048]
        ctx.sb["l1_Alog"] = G[4][0:128, 0:1024]
        ctx.sb["l1_Bp"] = G[2][0:128, 0:1024]
        ctx.sb["l1_Cp"] = G[3][0:128, 0:1024]
        ctx.sb_w["l1_inw"] = wtu[0:8]
        for j in range(8):
            ctx.sb[f"l3W{j}"] = G[j]
        ctx.sb["l3W8"] = None
        ctx.sb["l3W9"] = None
        ctx.sb["l3W10"] = G[8]
        ctx.sb["l3W11"] = G[9]
        for i in range(NDT):
            wj = G[i // 2]
        ctx.sb["l3_qi"] = ctx.sb["scrA"][0:128, 0:TOK].bitcast(I32)
        ctx.sb["l3_bfx"] = Gbf[8][0:128, 0:2048]
        ctx.sb["l3_Ae"] = G[9][0:128, 1024:1280]
        ctx.sb["l3_As"] = G[9][0:128, 1280:1536]
        for nm, xi in (("l3_Brb", 0), ("l3_Bib", 1), ("l3_Crb", 2),
                       ("l3_Cib", 3), ("l3_ut", 4), ("l3_dtt", 5),
                       ("l3_yt", 6), ("l3_du", 7)):
            ctx.sb[nm] = ctx.sb[f"xT{xi}"]
        ctx.sb_w["l3_w"] = wtu[0:16]
        ctx.sb_w["l3_ow"] = [Gbf[2][0:128, 0:1024]]

        for nm, shape, dt_ in (("l1_ybf", [16, 128, TOK], BF16),
                               ("l3_u", [16, 128, TOK], F32),
                               ("l3_ubf", [16, 128, TOK], BF16),
                               ("l3_zs", [16, 128, TOK], F32),
                               ("l3_dt", [16, 128, TOK], F32),
                               ("l3_dbf", [16, 128, TOK], BF16),
                               ("l3_bc", [64, TOK], F32),
                               ("l3_y", [16, 128, TOK], F32),
                               ("l3_ybf", [16, 128, TOK], BF16),
                               ("l3_xsp", [D, TOK], F32)):
            ctx.dram[nm] = nc.dram_tensor(nm + "_scr", shape, dt_,
                                          kind="Internal").ap()

        _setup_common(ctx, xin, consts_in, ident_in, want_ident=True)

        wg_bf = nc.dram_tensor("f_l0wg_bf", [8, D, D], BF16, kind="Internal").ap()
        wu_bf = nc.dram_tensor("f_l0wu_bf", [8, D, D], BF16, kind="Internal").ap()
        wd_bf = nc.dram_tensor("f_l0wd_bf", [8, D, D], BF16, kind="Internal").ap()
        for e in range(8):
            precast_to_dram(ctx, "l0wgbf", wi["l0_wg"][e], wg_bf[e], D, D)
            precast_to_dram(ctx, "l0wubf", wi["l0_wu"][e], wu_bf[e], D, D)
            precast_to_dram(ctx, "l0wdbf", wi["l0_wd"][e], wd_bf[e], D, D)
        mha0 = {}
        for short, nm in (("wq", "l0_wqT"), ("wk", "l0_wkT"),
                          ("wv", "l0_wvT"), ("wo", "l0_woT")):
            dr = nc.dram_tensor(f"f_{nm}_bf", [D, D], BF16,
                                kind="Internal").ap()
            precast_to_dram(ctx, f"{nm}bf", wi[nm], dr, D, D)
            mha0[f"{short}_bf"] = dr
            mha0[f"{short}_src"] = f"{nm}bf"
        w1bf = nc.dram_tensor("f_l1w1_bf", [16, D, 512], BF16,
                              kind="Internal").ap()
        w3bf = nc.dram_tensor("f_l1w3_bf", [16, D, 512], BF16,
                              kind="Internal").ap()
        w2bf = nc.dram_tensor("f_l1w2_bf", [16, 512, D], BF16,
                              kind="Internal").ap()
        for e in range(16):
            precast_to_dram(ctx, "l1w1bf", wi["l1_w1"][e], w1bf[e], D, 512)
            precast_to_dram(ctx, "l1w3bf", wi["l1_w3"][e], w3bf[e], D, 512)
            precast_to_dram(ctx, "l1w2bf", wi["l1_w2"][e], w2bf[e], 512, D)
        inwbf = nc.dram_tensor("f_l1inw_bf", [48, D, 128], BF16,
                               kind="Internal").ap()
        for t in range(48):
            precast_to_dram(ctx, "l1inwbf", wi["l1_inw"][t], inwbf[t], D, 128)
        outwbf = nc.dram_tensor("f_l1outw_bf", [2048, D], BF16,
                                kind="Internal").ap()
        precast_to_dram(ctx, "l1outwbf", wi["l1_outw"], outwbf, 2048, D)
        mha2 = {}
        for short, nm in (("wq", "l2_wqT"), ("wk", "l2_wkT"),
                          ("wv", "l2_wvT"), ("wo", "l2_woT")):
            dr = nc.dram_tensor(f"f_{nm}_bf", [D, D], BF16,
                                kind="Internal").ap()
            precast_to_dram(ctx, f"{nm}bf", wi[nm], dr, D, D)
            mha2[f"{short}_bf"] = dr
            mha2[f"{short}_src"] = f"{nm}bf"
        l3w1bf = nc.dram_tensor("f_l3w1T_bf", [16, D, 512], BF16,
                                kind="Internal").ap()
        l3w2bf = nc.dram_tensor("f_l3w2T_bf", [16, 512, D], BF16,
                                kind="Internal").ap()
        s1bf = nc.dram_tensor("f_l3s1_bf", [16, D, 128], BF16,
                              kind="Internal").ap()
        s2bf = nc.dram_tensor("f_l3s2_bf", [16, D, 128], BF16,
                              kind="Internal").ap()
        s3bf = nc.dram_tensor("f_l3s3_bf", [2048, D], BF16,
                              kind="Internal").ap()
        for e in range(16):
            precast_to_dram(ctx, "l3w1Tbf", wi["l3_w1T"][e], l3w1bf[e],
                            D, 512)
            precast_to_dram(ctx, "l3w2Tbf", wi["l3_w2T"][e], l3w2bf[e],
                            512, D)
            precast_to_dram(ctx, "sfc1Tbf", wi["l3_sfc1T"][e], s1bf[e],
                            D, 128)
            precast_to_dram(ctx, "sfc2Tbf", wi["l3_sfc2T"][e], s2bf[e],
                            D, 128)
        precast_to_dram(ctx, "sfc3Tbf", wi["l3_sfc3T"], s3bf, 2048, D)
        inpbf = nc.dram_tensor("f_l3inp_bf", [32, D, 128], BF16,
                               kind="Internal").ap()
        for t in range(32):
            precast_to_dram(ctx, "l3inpbf", wi["l3_inpT"][t], inpbf[t],
                            D, 128)
        xpbf = nc.dram_tensor("f_l3xp_bf", [17, 2048, 128], BF16,
                              kind="Internal").ap()
        for t in range(17):
            precast_to_dram(ctx, "l3xpbf", wi["l3_xpT"][t], xpbf[t],
                            2048, 128)
        dtpbf = nc.dram_tensor("f_l3dtp_bf", [16, 2048, 128], BF16,
                               kind="Internal").ap()
        for t in range(16):
            precast_to_dram(ctx, "l3dtpbf", wi["l3_dtpT"][t], dtpbf[t],
                            2048, 128)
        l3outbf = nc.dram_tensor("f_l3out_bf", [2048, D], BF16,
                                 kind="Internal").ap()
        precast_to_dram(ctx, "l3outbf", wi["l3_outT"], l3outbf, 2048, D)

        xT = [ctx.sb[f"xT{i}"] for i in range(NDT)]
        xb = _bufnames("xT", NDT)

        def blk_l0f(call):
            prog.fence()
            prog.dma(ctx.sb["l0_bg"][:], wi["l0_bg_in"], writes=["l0_bg"])
            prog.dma(ctx.sb["l0_bu"][:], wi["l0_bu_in"], writes=["l0_bu"])
            prog.dma(ctx.sb["l0_db"], wi["l0_db_in"][0:8, 0:1024],
                     writes=["l0_db"])
            for ti in range(NTT):
                prog.dma(ctx.sb["l0_idxi"][:, ti * 2:(ti + 1) * 2],
                         wi["l0_idx"][ti * 128:(ti + 1) * 128, :],
                         writes=["l0_idxi"])
                prog.dma(ctx.sb["l0_rwt"][:, ti * 8:(ti + 1) * 8],
                         wi["l0_rw"][ti * 128:(ti + 1) * 128, :],
                         writes=["l0_rwt"])
            prog.step("dve", lambda h: nc.vector.tensor_copy(
                out=ctx.sb["l0_idxf"][:], in_=ctx.sb["l0_idxi"][:]),
                reads=["l0_idxi"], writes=["l0_idxf"])
            wdr = {"wg_bf": wg_bf, "wu_bf": wu_bf, "wd_bf": wd_bf}
            emit_l0_ffn(ctx, xb, xT, call, wdr)

        def blk_l0a():
            prog.fence()
            for nm in ("l0_gq", "l0_bq", "l0_gk", "l0_bk"):
                prog.dma(ctx.sb[nm][:], wi[nm + "_in"], writes=[nm])
            prog.dma(ctx.sb["l0_bo"][:], wi["l0_bo_in"], writes=["l0_bo"])
            prog.dma(ctx.sb["mask4"][:], wi["mask4_in"], writes=["mask4"])
            prog.dma(ctx.sb["ind2"][:], wi["ind2_l0"], writes=["ind2"])
            ctx.mha_rowmap = [(0, 64, 0), (64, 128, 1)]
            cfg = dict(sm_scale=1.0,
                       norm_qk=make_norm_l0("l0_gq", "l0_bq", "l0_gk",
                                            "l0_bk"),
                       bo_col=ctx.sb["l0_bo"], **mha0)
            emit_mha(ctx, xb, xT, "l0a_", cfg)

        def blk_l1f(call):
            prog.fence()
            if call == 0:
                load_weight_bf16(ctx, "l1_gwT", wi["l1_gwT"], NDT, 16)
            ln_bf = [ctx.sb[f"bfA{i}"] for i in range(NDT)]
            ln_bufs = _bufnames("bfA", NDT)
            tag = f"l1f{call}_"
            emit_layernorm(ctx, xb, xT, ln_bf, ln_bufs, tag)
            emit_moe_gate(ctx, ln_bf, ln_bufs, "l1_gwT", 16, 4, tag)
            cfg = dict(n_exp=16, inter=512, use_w3=True, glue=glue_l1,
                       n_par=1,
                       w1_dram=w1bf, w1_src="l1w1bf",
                       w3_dram=w3bf, w3_src="l1w3bf",
                       w2_dram=w2bf, w2_src="l1w2bf")
            emit_moe_experts(ctx, xb, xT, ln_bf, ln_bufs, cfg, tag)

        def blk_l1a():
            prog.fence()
            for nm in ("l1_Dp", "l1_convw", "l1_convb", "l1_inb", "l1_outb",
                       "l1_gamma", "l1_beta"):
                prog.dma(ctx.sb[nm][:], wi[nm + "_in"], writes=[nm])
            for nm in ("l1_Alog", "l1_Bp", "l1_Cp"):
                prog.dma(ctx.sb[nm][0:128, :], wi[nm + "_in"], writes=[nm])
            wdr = {"inw_bf": inwbf, "outw_bf": outwbf}
            emit_l1_attn(ctx, xb, xT, wdr, "l1a_")

        def blk_l2f(call):
            prog.fence()
            tag = f"l2f{call}_"
            prog.dma(ctx.sb["l2_bias"][:], wi["l2_bias_in"],
                     writes=["l2_bias"])
            load_weight_bf16(ctx, "l2_wW", wi["l2_wW"], NDT, D,
                             reuse=(ctx.sb_w["l2_wW"], "l2_wW"))
            load_weight_bf16(ctx, "l2_wA1", wi["l2_wA1"], NDT, D,
                             reuse=(ctx.sb_w["l2_wA1"], "l2_wA1"))
            emit_l2_ffn(ctx, xb, xT, call, {"l2_wA2": wi["l2_wA2"]})

        def blk_l2a():
            prog.fence()
            for nm in ("l2_qw", "l2_kw"):
                prog.dma(ctx.sb[nm][:], wi[nm + "_in"], writes=[nm])
            prog.dma(ctx.sb["rope_cs2"][:], wi["rope_cos_in"],
                     writes=["rope_cs2"])
            prog.dma(ctx.sb["rope_sn2"][:], wi["rope_sin_in"],
                     writes=["rope_sn2"])
            prog.dma(ctx.sb["mask4"][:], wi["mask4_in"], writes=["mask4"])
            prog.dma(ctx.sb["ind2"][:], wi["ind2_l2"], writes=["ind2"])
            ctx.mha_rowmap = [(0, 64, 0), (64, 128, 1)]
            cfg = dict(sm_scale=0.125,
                       norm_qk=make_norm_l2("l2_qw", "l2_kw"),
                       bo_col=None, **mha2)
            emit_mha(ctx, xb, xT, "l2a_", cfg)

        def blk_l3f(call):
            prog.fence()
            if call == 0:
                load_weight_bf16(ctx, "l3_gwT", wi["l3_gwT"], NDT, 16)
            wdr = {"w1T_bf": l3w1bf, "w2T_bf": l3w2bf, "sfc1T_bf": s1bf,
                   "sfc2T_bf": s2bf, "sfc3T_bf": s3bf}
            emit_l3_ffn(ctx, xb, xT, call, wdr)

        def blk_l3a():
            prog.fence()
            for nm in ("l3_Alr", "l3_Ali", "l3_D", "l3_convw", "l3_convb",
                       "l3_dtb"):
                prog.dma(ctx.sb[nm][:], wi[nm + "_in"], writes=[nm])
            wdr = {"inpT_bf": inpbf, "xpT_bf": xpbf, "dtpT_bf": dtpbf,
                   "outT_bf": l3outbf}
            emit_l3_attn(ctx, xb, xT, wdr, "l3a_", fused=True)

        blk_l0f(0)
        blk_l0a()
        blk_l0f(1)
        blk_l1f(0)
        blk_l1a()
        blk_l1f(1)
        blk_l2f(0)
        blk_l2a()
        blk_l2f(1)
        blk_l3f(0)
        blk_l3a()
        blk_l3f(1)

        prog.fence()
        store_stream_outputs(ctx, xout)
        prog.finalize(stack)
    return nc


_PROG_CACHE = {}


def _cached(name, builder):
    if name not in _PROG_CACHE:
        _PROG_CACHE[name] = builder()
    return _PROG_CACHE[name]


def kernel(**inputs):
    _ = inputs.get("L0_attn_mask")
    x = np.ascontiguousarray(np.asarray(inputs["L0_attn_x"], np.float32))
    xs = [np.ascontiguousarray(x[c].T) for c in range(8)]

    w = {}
    w.update(stage_l0_ffn_weights(inputs))
    w.update(stage_l0_attn_weights(inputs))
    l2aw = stage_l2_attn_weights(inputs)
    w.update({k: v for k, v in l2aw.items() if k != "ind2"})
    w["ind2_l2v"] = l2aw["ind2"]
    w.update(stage_l1_ffn_weights(inputs))
    w.update(stage_l1_attn_weights(inputs))
    w.update(stage_l2_ffn_weights(inputs))
    w.update(stage_l3_ffn_weights(inputs))
    w.update(stage_l3_attn_weights(inputs))

    ren = {
        "l0_bg": "l0_bg_in", "l0_bu": "l0_bu_in", "l0_db": "l0_db_in",
        "l0_gq": "l0_gq_in", "l0_bq": "l0_bq_in", "l0_gk": "l0_gk_in",
        "l0_bk": "l0_bk_in", "l0_bo": "l0_bo_in",
        "mask4": "mask4_in", "ind2": "ind2_l0", "ind2_l2v": "ind2_l2",
        "l2_bias": "l2_bias_in", "l2_qw": "l2_qw_in", "l2_kw": "l2_kw_in",
        "rope_cos": "rope_cos_in", "rope_sin": "rope_sin_in",
        "l1_Alog": "l1_Alog_in", "l1_Bp": "l1_Bp_in", "l1_Cp": "l1_Cp_in",
        "l1_Dp": "l1_Dp_in", "l1_convw": "l1_convw_in",
        "l1_convb": "l1_convb_in", "l1_inb": "l1_inb_in",
        "l1_outb": "l1_outb_in", "l1_gamma": "l1_gamma_in",
        "l1_beta": "l1_beta_in",
        "l3_Alr": "l3_Alr_in", "l3_Ali": "l3_Ali_in", "l3_D": "l3_D_in",
        "l3_convw": "l3_convw_in", "l3_convb": "l3_convb_in",
        "l3_dtb": "l3_dtb_in",
    }
    consts = make_const_host()
    ident = make_ident_host()
    idx = np.ascontiguousarray(
        np.asarray(inputs["L0_ffn_router_indices"], np.int32))
    rw = np.ascontiguousarray(
        np.asarray(inputs["L0_ffn_routing_weights"], np.float32))
    zrw = np.zeros_like(rw)

    nc = _cached("fused", build_fused_program)
    in_maps = []
    for c in range(8):
        m = dict(xT=xs[c], const_in=consts, ident_in=ident, l0_idx=idx,
                 l0_rw=(rw if c == 0 else zrw))
        for k, v in w.items():
            m[ren.get(k, k)] = v
        in_maps.append(m)
    res = run_bass_kernel_spmd(nc, in_maps, core_ids=list(range(8)))
    out = np.stack([np.ascontiguousarray(np.asarray(r["xT_out"])).T
                    for r in res.results]).astype(np.float32)
    return out


def kernel_unfused(**inputs):
    x = np.ascontiguousarray(np.asarray(inputs["L0_attn_x"], np.float32))
    xs = [np.ascontiguousarray(x[c].T) for c in range(8)]

    w_l0f = stage_l0_ffn_weights(inputs)
    w_l0a = stage_l0_attn_weights(inputs)
    w_l1f = stage_l1_ffn_weights(inputs)
    w_l1a = stage_l1_attn_weights(inputs)
    w_l2f = stage_l2_ffn_weights(inputs)
    w_l2a = stage_l2_attn_weights(inputs)
    w_l3f = stage_l3_ffn_weights(inputs)
    w_l3a = stage_l3_attn_weights(inputs)
    idx = np.ascontiguousarray(
        np.asarray(inputs["L0_ffn_router_indices"], np.int32))
    rw = np.ascontiguousarray(
        np.asarray(inputs["L0_ffn_routing_weights"], np.float32))

    consts = make_const_host()
    ident = make_ident_host()

    def run(nc, maps):
        res = run_bass_kernel_spmd(nc, maps, core_ids=list(range(8)))
        return [np.ascontiguousarray(np.asarray(r["xT_out"])) for r in res.results]

    def mk_maps(weights, ren=(), extra=None):
        rend = dict(ren)
        out = []
        for c in range(8):
            m = dict(xT=xs[c], const_in=consts, ident_in=ident)
            for k, v in weights.items():
                m[rend.get(k, k)] = v
            if extra:
                for k, v in extra(c).items():
                    m[k] = v
            out.append(m)
        return out

    zrw = np.zeros_like(rw)

    def l0f():
        nonlocal xs
        nc = _cached("l0f", build_l0_ffn_program)
        ren = {"l0_bg": "l0_bg_in", "l0_bu": "l0_bu_in", "l0_db": "l0_db_in"}
        xs = run(nc, mk_maps(w_l0f, ren,
                             lambda c: {"l0_idx": idx,
                                        "l0_rw": (rw if c == 0 else zrw)}))

    def l0a():
        nonlocal xs
        nc = _cached("l0a", build_l0_attn_program)
        ren = {"mask4": "mask4_in", "ind2": "ind2_in", "l0_gq": "l0_gq_in",
               "l0_bq": "l0_bq_in", "l0_gk": "l0_gk_in", "l0_bk": "l0_bk_in",
               "l0_bo": "l0_bo_in"}
        xs = run(nc, mk_maps(w_l0a, ren))

    def l1f():
        nonlocal xs
        nc = _cached("l1f", build_l1_ffn_program)
        xs = run(nc, mk_maps(w_l1f))

    def l1a():
        nonlocal xs
        nc = _cached("l1a", build_l1_attn_program)
        ren = {k: k + "_in" for k in w_l1a if k not in ("l1_inw", "l1_outw")}
        xs = run(nc, mk_maps(w_l1a, ren))

    def l2f():
        nonlocal xs
        nc = _cached("l2f", build_l2_ffn_program)
        xs = run(nc, mk_maps(w_l2f, [("l2_bias", "l2_bias_in")]))

    def l2a():
        nonlocal xs
        nc = _cached("l2a", build_l2_attn_program)
        ren = {"mask4": "mask4_in", "ind2": "ind2_in",
               "rope_cos": "rope_cos_in", "rope_sin": "rope_sin_in",
               "l2_qw": "l2_qw_in", "l2_kw": "l2_kw_in"}
        xs = run(nc, mk_maps(w_l2a, ren))

    def l3f():
        nonlocal xs
        nc = _cached("l3f", build_l3_ffn_program)
        xs = run(nc, mk_maps(w_l3f))

    def l3a():
        nonlocal xs
        nc = _cached("l3a", build_l3_attn_program)
        ren = {k: k + "_in" for k in ("l3_Alr", "l3_Ali", "l3_D", "l3_convw",
                                      "l3_convb", "l3_dtb")}
        xs = run(nc, mk_maps(w_l3a, ren))

    for blk in (l0f, l0a, l0f, l1f, l1a, l1f, l2f, l2a, l2f,
                l3f, l3a, l3f):
        blk()

    out = np.stack([xs[c].T for c in range(8)]).astype(np.float32)
    return out

